# version 05
# baseline (speedup 1.0000x reference)
import numpy as np
import ml_dtypes
import concourse.bass as bass
import concourse.mybir as mybir
from concourse.bass_utils import run_bass_kernel_spmd
from contextlib import ExitStack

BFNP = ml_dtypes.bfloat16
F32 = mybir.dt.float32
BF16 = mybir.dt.bfloat16
AF = mybir.ActivationFunctionType
ALU = mybir.AluOpType
AX = mybir.AxisListType

NCORES = 8
B, S, HID = 4, 2048, 2048
H, D = 16, 128
E = 16
T = 1024
HALO = 128
TT_ = HALO + T
NK = HID // 128
SCALE = 1.0 / float(np.sqrt(D))
ALPHA = 1.702
KA = float(1.0 / ALPHA)
LIMIT = 7.0
G3 = [(0, 512), (512, 512), (1024, 128)]
CINV = 1.0 / HID

O_TT = 0
O_VF = 18432
O_OUT = 36864
O_TR = 53248
O_WB = 62976
O_SM = 75264
TR_Q2, TR_Q, TR_K2, TR_K, TR_VP, TR_SH, TR_AC = (O_TR + i * 1152 for i in range(7))
TR_SIG = O_TR + 7 * 1152
TR_T1 = TR_SIG + 512
TR_WRA = O_TR
TR_WRB = O_TR + 1024
TR_SZ = O_TR + 2048
TR_S1 = TR_SZ + 512
TR_SQ2 = TR_S1 + 512
TR_LN = O_TR
TR_T2 = O_TR + 2048
TR_AT = O_TR + 4096
TR_OBI = O_TR + 6144
TR_GB = O_TR
TR_SA = O_TR + 4096
TR_BB = TR_SA + 512
TR_Q2G = TR_BB + 512
TR_GBB = TR_Q2G + 512
TR_B2 = TR_GBB + 1024
TR_GT = O_TR + 8704
SM_E16 = O_SM
SM_ID = O_SM + 256
SM_ONESM = O_SM + 384
SM_GW = O_SM + 512
SM_ONESR = O_SM + 768
SM_WAB = O_SM + 896
SM_EONE = O_SM + 1408
SH2 = O_SM + 1632
BF_COLS = O_SM + 2784

F_XB = 0
F_ROWS = 4096
F_RR = 6400
F_CONV = 7424
F_NW = 7616
F_STAT = 7648
F_MISC = 7680
F_F1 = 7744
F_F2 = F_RR
F_B1 = 8256
F_D = 9024
F_X1 = F_D
F32_COLS = F_D + 2560
F_G = F_RR
F_GS = F_RR + 128
F_YS = F_XB


class Sem:
    def __init__(self, es, nc, name):
        self.h = es.enter_context(nc.semaphore(name))
        self.n = 0


class Emit:
    KEYS = ("tensor", "vector", "scalar", "gpsimd", "sync")

    def __init__(self):
        self.q = {k: [] for k in self.KEYS}
        self.selfs = None
        self.marker = {}

    def add(self, key, fn, waits=(), inc=None, incn=1, dma=False):
        waits = [(s, v) for (s, v) in waits if v > 0]
        chain = self.selfs is not None and key in self.selfs and not dma
        selfw = self.marker.get(key) if chain else None
        use_self = chain and inc is None
        if chain:
            if inc is None:
                ss = self.selfs[key]
                ss.n += 1
                self.marker[key] = (ss, ss.n)
            else:
                self.marker[key] = (inc, inc.n + incn)

        def f(eng, waits=waits, fn=fn, inc=inc, incn=incn, selfw=selfw,
              use_self=use_self, key=key):
            for (s, v) in waits:
                eng.wait_ge(s.h, v)
            if selfw is not None:
                eng.wait_ge(selfw[0].h, selfw[1])
            ins = fn(eng)
            if ins is None:
                ins = eng.nop()
            if inc is not None:
                ins.then_inc(inc.h, incn)
            if use_self:
                ins.then_inc(self.selfs[key].h, 1)
        self.q[key].append(f)
        if inc is not None:
            inc.n += incn


def build(dbg=None, selfchain=False):
    if selfchain is True:
        selfchain = ("tensor", "vector", "scalar")
    elif not selfchain:
        selfchain = ()
    nc = bass.Bass("TRN2", target_bir_lowering=False,
                   detect_race_conditions="tensor" in selfchain)

    d_x = nc.dram_tensor("xsh", [TT_, HID], F32, kind="ExternalInput").ap()
    d_wq = nc.dram_tensor("wq_t", [NK, 128, 2048], F32, kind="ExternalInput").ap()
    d_wk = nc.dram_tensor("wk_t", [NK, 128, 2048], F32, kind="ExternalInput").ap()
    d_wv = nc.dram_tensor("wv_t", [NK, 128, 2048], F32, kind="ExternalInput").ap()
    d_wz = nc.dram_tensor("wz_t", [NK, 128, 2048], F32, kind="ExternalInput").ap()
    d_wo = nc.dram_tensor("wo_t", [NK, 128, 2048], F32, kind="ExternalInput").ap()
    d_wab = nc.dram_tensor("wab_t", [128, 512], F32, kind="ExternalInput").ap()
    d_conv = nc.dram_tensor("conv_t", [128, 192], F32, kind="ExternalInput").ap()
    d_nw = nc.dram_tensor("nw_t", [128, 32], F32, kind="ExternalInput").ap()
    d_cos = nc.dram_tensor("cos_t", [128, TT_], F32, kind="ExternalInput").ap()
    d_s2 = nc.dram_tensor("s2_t", [128, TT_], F32, kind="ExternalInput").ap()
    d_cb = nc.dram_tensor("cb_t", [128, 736], BF16, kind="ExternalInput").ap()
    d_er = nc.dram_tensor("er_t", [16, 2048], BF16, kind="ExternalInput").ap()
    d_gw = nc.dram_tensor("gw_t", [128, 256], F32, kind="ExternalInput").ap()
    d_gb = nc.dram_tensor("gb_t", [1, 16], F32, kind="ExternalInput").ap()
    d_b1 = nc.dram_tensor("b1_t", [128, 512], F32, kind="ExternalInput").ap()
    d_b2 = nc.dram_tensor("b2_t", [16, 2048], F32, kind="ExternalInput").ap()
    d_w1a = nc.dram_tensor("w1a_t", [E, NK, 128, 2048], F32, kind="ExternalInput").ap()
    d_w1b = nc.dram_tensor("w1b_t", [E, NK, 128, 2048], F32, kind="ExternalInput").ap()
    d_w2 = nc.dram_tensor("w2_t", [E, NK, 128, 2048], F32, kind="ExternalInput").ap()
    d_y = nc.dram_tensor("y", [T, HID], F32, kind="ExternalOutput").ap()
    d_yx1 = None
    if dbg == "x1":
        d_yx1 = nc.dram_tensor("yx1", [T, HID], F32, kind="ExternalOutput").ap()

    es = ExitStack()
    bf = es.enter_context(nc.sbuf_tensor("bfA", [128, BF_COLS], BF16))
    fA = es.enter_context(nc.sbuf_tensor("f32A", [128, F32_COLS], F32))
    er_sb = es.enter_context(nc.sbuf_tensor("er_sb", [16, 2048], BF16))
    gb_sb = es.enter_context(nc.sbuf_tensor("gb_sb", [1, 16], F32))
    gbb_sb = es.enter_context(nc.sbuf_tensor("gbb_sb", [1, 16], BF16))
    pb = [es.enter_context(nc.psum_tensor(f"pb{i}", [128, 512], F32)) for i in range(7)]
    ptr = es.enter_context(nc.psum_tensor("ptr", [128, 1024], BF16))
    vf32 = bf.bitcast(F32)

    def bfr(off, n):
        return bf[:, off:off + n]

    def fr(off, n):
        return fA[:, off:off + n]

    def wbuf(i):
        return bf[:, O_WB + 2048 * i:O_WB + 2048 * (i + 1)]

    sWs = [Sem(es, nc, f"sW{i}") for i in range(6)]
    sX0 = Sem(es, nc, "sX0")
    sX1 = Sem(es, nc, "sX1")
    sXR = Sem(es, nc, "sXR")
    sB2 = Sem(es, nc, "sB2")
    sT = Sem(es, nc, "sT")
    sV = Sem(es, nc, "sV")
    sS = Sem(es, nc, "sS")
    sST = Sem(es, nc, "sST")
    sGC = Sem(es, nc, "sGC")

    em = Emit()
    if selfchain:
        selfs = {k: Sem(es, nc, f"self_{k}") for k in selfchain}
        em.selfs = selfs
    T_ = lambda fn, waits=(), inc=None, incn=1: em.add("tensor", fn, waits, inc, incn)
    V_ = lambda fn, waits=(), inc=None, incn=1: em.add("vector", fn, waits, inc, incn)
    S_ = lambda fn, waits=(), inc=None, incn=1: em.add("scalar", fn, waits, inc, incn)
    G_ = lambda fn, waits=(), inc=None, incn=1: em.add("gpsimd", fn, waits, inc, incn,
                                                      dma=True)
    GC_ = lambda fn, waits=(), inc=None, incn=1: em.add("gpsimd", fn, waits, inc, incn)
    Y_ = lambda fn, waits=(), inc=None, incn=1: em.add("sync", fn, waits, inc, incn,
                                                      dma=True)

    ident = bfr(SM_ID, 128)
    E16 = bfr(SM_E16, 256)
    onesmat = bfr(SM_ONESM, 128)
    onesrow = bfr(SM_ONESR, 128)[0:1, :]
    wab_bf = bfr(SM_WAB, 512)
    gwb = bfr(SM_GW, 256)
    stats = fr(F_STAT, 32)
    convw = fr(F_CONV, 192)
    C_A = F_ROWS
    C_B = F_ROWS + 1152
    cosbf = bfr(O_OUT, TT_)
    s2bf = bfr(O_OUT + TT_, TT_)
    sqsc = bfr(TR_T2, 2048)

    Y_(lambda e: e.dma_start(out=bfr(SM_E16, 512), in_=d_cb[:, 0:512]), inc=sX0, incn=16)
    Y_(lambda e: e.dma_start(out=bfr(SM_EONE, 224), in_=d_cb[:, 512:736]), inc=sX0, incn=16)
    Y_(lambda e: e.dma_start(out=er_sb[:, :], in_=d_er[:, :]), inc=sX0, incn=16)
    VFF = O_VF // 2
    Y_(lambda e: e.dma_start(out=vf32[:, VFF:VFF + TT_], in_=d_cos[:, :]), inc=sX0, incn=16)
    Y_(lambda e: e.dma_start(out=vf32[:, VFF + TT_:VFF + 2 * TT_], in_=d_s2[:, :]), inc=sX0, incn=16)
    Y_(lambda e: e.dma_start(out=convw, in_=d_conv[:, :]), inc=sX0, incn=16)
    Y_(lambda e: e.dma_start(out=fr(F_NW, 32), in_=d_nw[:, :]), inc=sX0, incn=16)
    Y_(lambda e: e.dma_start(out=fr(F_XB, 512), in_=d_wab[:, :]), inc=sX0, incn=16)
    Y_(lambda e: e.dma_start(out=fr(F_XB + 512, 256), in_=d_gw[:, :]), inc=sX0, incn=16)
    Y_(lambda e: e.dma_start(out=gb_sb[:, :], in_=d_gb[:, :]), inc=sX0, incn=16)
    Y_(lambda e: e.dma_start(out=fr(F_B1, 512), in_=d_b1[:, :]), inc=sX0, incn=16)
    nLoads = sX0.n

    V_(lambda e: e.tensor_copy(cosbf, vf32[:, VFF:VFF + TT_]), waits=[(sX0, nLoads)])
    V_(lambda e: e.tensor_copy(s2bf, vf32[:, VFF + TT_:VFF + 2 * TT_]))
    V_(lambda e: e.tensor_copy(wab_bf, fr(F_XB, 512)))
    V_(lambda e: e.tensor_copy(gwb, fr(F_XB + 512, 256)))
    V_(lambda e: e.tensor_copy(gbb_sb[:, :], gb_sb[:, :]))
    V_(lambda e: e.memset(bfr(SM_ONESR, 128)[0:1, :], 1.0))
    V_(lambda e: e.memset(fr(F_MISC, 1), 1e-5))
    V_(lambda e: e.memset(fr(F_MISC + 1, 1), 1e-6))
    V_(lambda e: e.memset(fr(F_MISC + 2, 1), ALPHA * LIMIT))
    V_(lambda e: e.tensor_scalar(fr(F_B1 + 512, 256), fr(F_B1, 256), -1.0, LIMIT,
                                 ALU.mult, ALU.add))
    V_(lambda e: e.memset(fr(F_D, 2560), 0.0), inc=sV)
    nConstV = sV.n
    ceps5 = fr(F_MISC, 1)
    ceps6 = fr(F_MISC + 1, 1)
    csilu = fr(F_MISC + 2, 1)
    c7ma = fr(F_B1 + 512, 256)
    b1b = fr(F_B1 + 256, 256)

    ptrB = pb[0].bitcast(BF16)
    tbanks = [ptr, ptrB]
    ptr_reads = [(sV, 0), (sV, 0)]
    ptr_rot = [0]
    last_transpose = [0]

    def ln_stats(src_ap, eps_ap, extra_waits=()):
        S_(lambda e: e.activation(src_ap, src_ap, AF.Copy, accum_out=stats[:, 0:1]),
           waits=list(extra_waits), inc=sS)
        S_(lambda e: e.activation(sqsc, src_ap, AF.Square, accum_out=stats[:, 1:2]),
           inc=sS)
        V_(lambda e: e.tensor_scalar(stats[:, 2:3], stats[:, 0:1], CINV, None, ALU.mult),
           waits=[(sS, sS.n)])
        V_(lambda e: e.tensor_scalar(stats[:, 3:4], stats[:, 1:2], CINV, None, ALU.mult))
        V_(lambda e: e.tensor_tensor(stats[:, 6:7], stats[:, 2:3], stats[:, 2:3], ALU.mult))
        V_(lambda e: e.tensor_tensor(stats[:, 3:4], stats[:, 3:4], stats[:, 6:7], ALU.subtract),
           inc=sV)
        S_(lambda e: e.activation(stats[:, 7:8], stats[:, 3:4], AF.Sqrt,
                                  bias=eps_ap[:, 0:1], scale=1.0),
           waits=[(sV, sV.n)], inc=sS)
        V_(lambda e: e.reciprocal(stats[:, 4:5], stats[:, 7:8]), waits=[(sS, sS.n)])
        V_(lambda e: e.scalar_tensor_tensor(stats[:, 5:6], stats[:, 2:3], -1.0, stats[:, 4:5],
                                            ALU.mult, ALU.mult), inc=sV)

    def transposes(src_bf, dst_writer, src_sem, src_val, wsem, extra0=()):
        for kk in range(NK):
            slot = kk % 8
            b = ptr_rot[0] % 2
            bank = tbanks[b]
            wts = [(src_sem, src_val)] + list(extra0) if kk == 0 else []
            if kk % 8 == 0:
                wts = wts + [ptr_reads[b]]
            T_(lambda e, kk=kk, slot=slot, src_bf=src_bf, bank=bank:
               e.transpose(bank[:, 128 * slot:128 * slot + 128],
                           src_bf[:, 128 * kk:128 * (kk + 1)], ident),
               waits=wts, inc=(sT if kk % 8 == 7 else None))
            if kk % 8 == 7:
                tcount = sT.n
                for q in range(8):
                    dst_writer(kk - 7 + q, bank[:, 128 * q:128 * q + 128],
                               [(sT, tcount)] if q == 0 else [], q == 7)
                ptr_reads[b] = (wsem, wsem.n)
                ptr_rot[0] += 1
        last_transpose[0] = sT.n

    xb_users_v = [nConstV, nConstV]
    xb_users_s = [0, 0]
    T_(lambda e: e.nop(), waits=[(sX0, nLoads)])
    for i in range(9):
        xslot = i % 2
        sXp = sX0 if xslot == 0 else sX1
        xb = fr(F_XB + 2048 * xslot, 2048)
        Y_(lambda e, i=i, xb=xb: e.dma_start(out=xb, in_=d_x[128 * i:128 * (i + 1), :]),
           waits=[(sV, xb_users_v[xslot]), (sS, xb_users_s[xslot])], inc=sXp, incn=16)
        ln_stats(xb, ceps5, extra_waits=[(sXp, sXp.n), (sS, sS.n)])
        lnbf = bfr(TR_LN, 2048)
        V_(lambda e, xb=xb: e.tensor_scalar(lnbf, xb, stats[:, 4:5], stats[:, 5:6],
                                            ALU.mult, ALU.add),
           waits=[(sT, last_transpose[0])], inc=sV)
        xb_users_v[xslot] = sV.n

        def wr_tt(kq, blk, wts, do_inc, i=i):
            S_(lambda e, kq=kq, blk=blk:
               e.copy(bfr(O_TT + 1152 * kq + 128 * i, 128), blk),
               waits=wts, inc=(sS if do_inc else None))
        transposes(lnbf, wr_tt, sV, sV.n, sS)

    ab_evac = [sV.n]
    for gi, (off, n) in enumerate(G3):
        for k in range(NK):
            T_(lambda e, k=k, off=off, n=n:
               e.matmul(pb[5][0:16, 0:n], wab_bf[:, 16 * k:16 * k + 16],
                        bfr(O_TT + 1152 * k + off, n), start=(k == 0), stop=(k == NK - 1)),
               waits=([(sV, sV.n), (sS, sS.n)] if (gi == 0 and k == 0) else
                      [(sV, ab_evac[0])] if k == 0 else []))
            T_(lambda e, k=k, off=off, n=n:
               e.matmul(pb[6][0:16, 0:n], wab_bf[:, 256 + 16 * k:256 + 16 * k + 16],
                        bfr(O_TT + 1152 * k + off, n), start=(k == 0), stop=(k == NK - 1)),
               inc=(sT if k == NK - 1 else None))
        V_(lambda e, off=off, n=n: e.tensor_copy(fA[0:16, C_A + off:C_A + off + n],
                                                 pb[5][0:16, 0:n]),
           waits=[(sT, sT.n)])
        V_(lambda e, off=off, n=n: e.tensor_copy(fA[0:16, C_B + off:C_B + off + n],
                                                 pb[6][0:16, 0:n]),
           inc=sV)
        ab_evac[0] = sV.n
    ab_free = sV.n

    q2b, qb, k2b, kb, vpre = (bfr(o, 1152) for o in (TR_Q2, TR_Q, TR_K2, TR_K, TR_VP))
    shb = bfr(TR_SH, 1152)
    acc1 = bfr(TR_AC, 1152)
    sigb = bfr(TR_SIG, 512)
    t1b = bfr(TR_T1, 512)
    D1, D2, D3, D4, D5 = (fr(F_D + 512 * i, 512) for i in range(5))

    wb_last_use = [0] * 6
    dma_ready = {}
    slot_of = {}
    order = [(h, w) for h in range(H) for w in ('v', 'q', 'k')]
    issued = [0]

    def wload_slot(slot, dram_ap):
        G_(lambda e, dram_ap=dram_ap, slot=slot: e.dma_start(out=wbuf(slot), in_=dram_ap),
           waits=[(sT, wb_last_use[slot])], inc=sWs[slot], incn=16)
        return (sWs[slot], sWs[slot].n)

    def issue_next():
        if issued[0] >= len(order):
            return
        h, which = order[issued[0]]
        slot = issued[0] % 6
        slot_of[(h, which)] = slot
        dma_ready[(h, which)] = wload_slot(slot, {'v': d_wv, 'q': d_wq, 'k': d_wk}[which][h])
        issued[0] += 1

    for _ in range(6):
        issue_next()

    proj_bank_free = [sV.n, sV.n]
    sig_free = [sV.n]
    GC_(lambda e: e.memset(shb[:, 0:128], 0.0), inc=sGC)

    pbc = [0]

    def proj(h, which, dest, war_gc=0):
        rdy = dma_ready[(h, which)]
        slot = slot_of[(h, which)]
        for gi, (off, n) in enumerate(G3):
            bi = pbc[0] % 2
            pbc[0] += 1
            bank = pb[bi]
            for k in range(NK):
                wts = [rdy, (sV, proj_bank_free[bi])] if k == 0 else []
                T_(lambda e, k=k, off=off, n=n, bank=bank, slot=slot:
                   e.matmul(bank[:, 0:n], wbuf(slot)[:, 128 * k:128 * (k + 1)],
                            bfr(O_TT + 1152 * k + off, n), start=(k == 0), stop=(k == NK - 1)),
                   waits=wts, inc=(sT if k == NK - 1 else None))
            S_(lambda e, bank=bank, n=n: e.activation(sigb[:, 0:n], bank[:, 0:n], AF.Sigmoid),
               waits=[(sT, sT.n), (sV, sig_free[0])], inc=sS)
            V_(lambda e, bank=bank, off=off, n=n, dest=dest:
               e.tensor_tensor(dest[:, off:off + n], bank[:, 0:n], sigb[:, 0:n], ALU.mult),
               waits=[(sS, sS.n)] + ([(sGC, war_gc)] if gi == 0 else []), inc=sV)
            proj_bank_free[bi] = sV.n
            sig_free[0] = sV.n
        wb_last_use[slot] = sT.n
        return sV.n

    def emit_conv(h, src, dst, wsel, rope_prev_gc=0, rope_dma_prev=0):
        base = 64 * wsel + 4 * h
        n = TT_ - 125
        w0 = ([(sGC, rope_prev_gc)] if rope_prev_gc else []) + \
             ([(sXR, rope_dma_prev)] if rope_dma_prev else [])
        V_(lambda e, src=src, dst=dst, base=base, n=n: e.tensor_scalar(
            dst[:, 125:125 + n], src[:, 122:122 + n], convw[:, base:base + 1],
            None, ALU.mult), waits=w0)
        V_(lambda e, src=src, dst=dst, base=base, n=n: e.scalar_tensor_tensor(
            dst[:, 125:125 + n], src[:, 123:123 + n], convw[:, base + 1:base + 2],
            dst[:, 125:125 + n], ALU.mult, ALU.add))
        V_(lambda e, src=src, dst=dst, base=base, n=n: e.scalar_tensor_tensor(
            dst[:, 125:125 + n], src[:, 124:124 + n], convw[:, base + 2:base + 3],
            dst[:, 125:125 + n], ALU.mult, ALU.add))
        V_(lambda e, src=src, dst=dst, base=base, n=n: e.scalar_tensor_tensor(
            dst[:, 125:125 + n], src[:, 125:125 + n], convw[:, base + 3:base + 4],
            dst[:, 125:125 + n], ALU.mult, ALU.add), inc=sV)
        return sV.n

    def emit_rope(src, dst, conv_done):
        n = TT_ - 125
        Y_(lambda e, src=src: e.dma_start(out=acc1[0:64, 125:125 + n],
                                          in_=src[64:128, 125:125 + n]),
           waits=[(sGC, sGC.n), (sV, conv_done)], inc=sXR, incn=16)
        Y_(lambda e, src=src: e.dma_start(out=acc1[64:128, 125:125 + n],
                                          in_=src[0:64, 125:125 + n]),
           inc=sXR, incn=16)
        GC_(lambda e, src=src, dst=dst: e.tensor_tensor(
            dst[:, 125:125 + n], src[:, 125:125 + n], cosbf[:, 125:125 + n], ALU.mult),
            waits=[(sV, conv_done)], inc=sGC)
        g1 = sGC.n
        GC_(lambda e: e.tensor_tensor(acc1[:, 125:125 + n], acc1[:, 125:125 + n],
                                      s2bf[:, 125:125 + n], ALU.mult),
            waits=[(sXR, sXR.n)])
        GC_(lambda e, dst=dst: e.tensor_tensor(dst[:, 125:125 + n], dst[:, 125:125 + n],
                                               acc1[:, 125:125 + n], ALU.add), inc=sGC)
        return g1, sXR.n

    dscr_free = [sV.n]

    g1q = [0]
    g1k = [0]
    ropeq_dma = [0]
    ropek_dma = [0]
    dots_done = [0]
    last_scr_dots_t = [0]
    pending = [None]
    shb2 = bfr(SH2, 1152)
    scr_total = {2: 3, 3: 3, 4: 3, 5: 3, 6: 2}
    DOTMAP2 = {(0, 0): (2, 0), (0, 1): (2, 1), (1, 0): (2, 2), (1, 1): (3, 0),
               (2, 0): (3, 1), (2, 1): (3, 2), (3, 0): (4, 0), (3, 1): (4, 1)}
    nn = TT_ - 125

    def make_scr(h):
        state = {"seen": {2: 0, 3: 0, 4: 0, 5: 0, 6: 0}, "first": True}

        def scr_mm(bank_i, slot, rhs_off, n, gc_val, src=shb):
            wts = [(sGC, gc_val)]
            if state["first"]:
                wts.append((sV, dscr_free[0]))
                if h == 0:
                    wts.append((sV, ab_free))
                state["first"] = False
            st = state["seen"][bank_i] == 0
            sp = state["seen"][bank_i] == scr_total[bank_i] - 1
            state["seen"][bank_i] += 1
            c0 = SM_EONE + 128 - (32 * slot + h)
            T_(lambda e, bank_i=bank_i, rhs_off=rhs_off, n=n, st=st, sp=sp, c0=c0,
               src=src:
               e.matmul(pb[bank_i][0:96, 0:n], bfr(c0, 96), src[:, rhs_off:rhs_off + n],
                        start=st, stop=sp, skip_group_check=True),
               waits=wts, inc=sT)
        return scr_mm

    for h in range(H):
        proj(h, 'v', vpre)
        issue_next()
        emit_conv(h, vpre, bf[:, O_VF + 1152 * h:O_VF + 1152 * (h + 1)], 2)
        if pending[0] is not None:
            pending[0]()
            pending[0] = None
        eq = proj(h, 'q', q2b, war_gc=dots_done[0])
        issue_next()
        cq = emit_conv(h, q2b, qb, 0, g1q[0], ropeq_dma[0])
        g1q[0], ropeq_dma[0] = emit_rope(qb, q2b, cq)
        GC_(lambda e: e.tensor_tensor(shb[:, 125:125 + nn], q2b[:, 125:125 + nn],
                                      q2b[:, 125:125 + nn], ALU.mult),
            waits=[(sT, sT.n)], inc=sGC)
        qq_done = sGC.n
        ek = proj(h, 'k', k2b, war_gc=dots_done[0])
        issue_next()
        ck = emit_conv(h, k2b, kb, 1, g1k[0], ropek_dma[0])
        g1k[0], ropek_dma[0] = emit_rope(kb, k2b, ck)
        scr = make_scr(h)
        scr(4, 2, 0, 512, qq_done)
        scr(5, 0, 512, 512, qq_done)
        scr(5, 1, 1024, 128, qq_done)
        scrqq_t = sT.n
        GC_(lambda e: e.tensor_tensor(shb[:, 125:125 + nn], k2b[:, 125:125 + nn],
                                      k2b[:, 125:125 + nn], ALU.mult),
            waits=[(sT, scrqq_t)], inc=sGC)
        kk_done = sGC.n

        def tail(h=h, scr=scr, kk_done=kk_done):
            scr(6, 0, 0, 512, kk_done)
            scr(6, 1, 512, 512, kk_done)
            scr(5, 2, 1024, 128, kk_done)
            prev_t = last_scr_dots_t[0]
            for s in range(4):
                GC_(lambda e, s=s: e.tensor_tensor(
                    shb2[:, 128:128 + T], q2b[:, 128:128 + T],
                    k2b[:, 128 - s:128 - s + T], ALU.mult),
                    waits=[(sT, prev_t)], inc=sGC)
                dval = sGC.n
                for half in range(2):
                    bank_i, slot = DOTMAP2[(s, half)]
                    scr(bank_i, slot, 128 + 512 * half, 512, dval, src=shb2)
                prev_t = sT.n
            last_scr_dots_t[0] = sT.n
            dots_done[0] = sGC.n
            em.add("vector", lambda e: e.nop(), waits=[(sT, sT.n)])
            V_(lambda e: e.tensor_tensor(D1[0:96, :], D1[0:96, :], pb[2][0:96, 0:512],
                                         ALU.add))
            V_(lambda e: e.tensor_tensor(D2[0:96, :], D2[0:96, :], pb[3][0:96, 0:512],
                                         ALU.add))
            V_(lambda e: e.tensor_tensor(D3[0:96, :], D3[0:96, :], pb[4][0:96, 0:512],
                                         ALU.add))
            V_(lambda e: e.tensor_tensor(D4[0:96, :], D4[0:96, :], pb[5][0:96, 0:512],
                                         ALU.add))
            V_(lambda e: e.tensor_tensor(D5[0:96, :], D5[0:96, :], pb[6][0:96, 0:512],
                                         ALU.add), inc=sV)
            dscr_free[0] = sV.n
        pending[0] = tail
    pending[0]()
    pending[0] = None
    nA3_V = sV.n
    nA3_GC = sGC.n

    rqsrc = [(D3, 64, 0, 512), (D4, 0, 512, 512), (D4, 32, 1024, 128)]
    rksrc = [(D5, 0, 0, 512), (D5, 32, 512, 512), (D4, 64, 1024, 128)]
    for cbase, regions in ((C_A, rqsrc), (C_B, rksrc)):
        for (dsrc, po, off, n) in regions:
            S_(lambda e, dsrc=dsrc, po=po, n=n: e.activation(
                fA[po:po + 16, F_F1:F_F1 + n], dsrc[po:po + 16, 0:n], AF.Sqrt),
               waits=[(sV, sV.n)], inc=sS)
            V_(lambda e, po=po, n=n: e.tensor_scalar(
                fA[po:po + 16, F_F1:F_F1 + n], fA[po:po + 16, F_F1:F_F1 + n],
                1e-12, None, ALU.max), waits=[(sS, sS.n)])
            V_(lambda e, cbase=cbase, po=po, off=off, n=n:
               e.reciprocal(fA[32:48, cbase + off:cbase + off + n],
                            fA[po:po + 16, F_F1:F_F1 + n]), inc=sV)

    def wrow(s):
        return bf[0:16, O_TR + 1024 * s:O_TR + 1024 * (s + 1)]

    DOTMAP = {(0, 0): (D1, 0), (0, 1): (D1, 32), (1, 0): (D1, 64),
              (1, 1): (D2, 0), (2, 0): (D2, 32), (2, 1): (D2, 64),
              (3, 0): (D3, 0), (3, 1): (D3, 32)}
    for s in range(4):
        for half in range(2):
            dsrc, po = DOTMAP[(s, half)]
            hb = 128 + 512 * half
            V_(lambda e, s=s, hb=hb, po=po: e.tensor_tensor(
                fA[po:po + 16, F_F1:F_F1 + 512],
                fA[0:16, C_A + hb - s:C_A + hb - s + 512],
                fA[0:16, C_B + hb:C_B + hb + 512], ALU.mult), inc=sV)
            S_(lambda e, po=po: e.activation(fA[po:po + 16, F_F1:F_F1 + 512],
                                             fA[po:po + 16, F_F1:F_F1 + 512], AF.Sigmoid),
               waits=[(sV, sV.n)], inc=sS)
            V_(lambda e, dsrc=dsrc, po=po: e.tensor_tensor(
                fA[po:po + 16, F_F2:F_F2 + 512], fA[po:po + 16, F_F1:F_F1 + 512],
                dsrc[po:po + 16, 0:512], ALU.mult), waits=[(sS, sS.n)])
            V_(lambda e, s=s, hb=hb, po=po: e.scalar_tensor_tensor(
                fA[po:po + 16, F_F1:F_F1 + 512],
                fA[32:48, C_A + hb:C_A + hb + 512], SCALE,
                fA[32:48, C_B + hb - s:C_B + hb - s + 512], ALU.mult, ALU.mult))
            V_(lambda e, s=s, half=half, po=po: e.tensor_tensor(
                wrow(s)[:, 512 * half:512 * half + 512],
                fA[po:po + 16, F_F2:F_F2 + 512], fA[po:po + 16, F_F1:F_F1 + 512],
                ALU.mult), inc=sV)
    nA4_V = sV.n

    aout = bfr(O_OUT, 16384)
    p2_free = [nA4_V, nA4_V]
    for h in range(H):
        for s in range(4):
            wsrc = wrow(s)
            for half in range(2):
                bi = (2 * s + half) % 2
                bank = pb[bi]
                T_(lambda e, bank=bank, h=h, wsrc=wsrc, half=half:
                   e.matmul(bank[:, 0:512], er_sb[:, 128 * h:128 * (h + 1)],
                            wsrc[:, 512 * half:512 * half + 512], start=True, stop=True),
                   waits=[(sV, p2_free[bi])], inc=sT)
                vcol = O_VF + 1152 * h + 128 + 512 * half - s
                ocol = O_OUT + 1024 * h + 512 * half
                if s == 0:
                    V_(lambda e, bank=bank, vcol=vcol, ocol=ocol:
                       e.tensor_tensor(bf[:, ocol:ocol + 512], bank[:, 0:512],
                                       bf[:, vcol:vcol + 512], ALU.mult),
                       waits=[(sT, sT.n)], inc=sV)
                else:
                    V_(lambda e, bank=bank, vcol=vcol:
                       e.tensor_tensor(t1b[:, 0:512], bank[:, 0:512],
                                       bf[:, vcol:vcol + 512], ALU.mult),
                       waits=[(sT, sT.n)])
                    V_(lambda e, ocol=ocol: e.tensor_tensor(
                        bf[:, ocol:ocol + 512], bf[:, ocol:ocol + 512], t1b[:, 0:512],
                        ALU.add), inc=sV)
                p2_free[bi] = sV.n
    nA5_V = sV.n

    rr = fr(F_RR, 1024)
    sqb2 = bfr(TR_SQ2, 512)
    rr_mm_t = [0]
    for g in range(2):
        for ch in range(NK):
            S_(lambda e, ch=ch, g=g: e.activation(
                sqb2[:, 0:512], aout[:, 1024 * ch + 512 * g:1024 * ch + 512 * g + 512],
                AF.Square),
               waits=[(sV, nA5_V), (sT, rr_mm_t[0])], inc=sS)
            T_(lambda e, ch=ch: e.matmul(pb[2][:, 0:512], onesmat, sqb2[:, 0:512],
                                         start=(ch == 0), stop=(ch == NK - 1)),
               waits=[(sS, sS.n)] + ([(sV, sV.n)] if (g == 1 and ch == 0) else []),
               inc=sT)
            rr_mm_t[0] = sT.n
        S_(lambda e: e.activation(fr(F_F1, 512)[:, 0:512], pb[2][:, 0:512], AF.Sqrt,
                                  bias=ceps6[:, 0:1], scale=CINV),
           waits=[(sT, sT.n)], inc=sS)
        V_(lambda e, g=g: e.reciprocal(rr[:, 512 * g:512 * g + 512], fr(F_F1, 512)[:, 0:512]),
           waits=[(sS, sS.n)], inc=sV)

    out2 = bfr(O_VF, 16384)
    nwa = fr(F_NW, 32)
    szc = bfr(TR_SZ, 512)
    s1c = bfr(TR_S1, 512)
    z_bank_free = [sV.n, sV.n]
    zrdy = {}
    zslot = {}
    zq = [0]

    def z_issue():
        ci = zq[0]
        if ci >= NK:
            return
        slot = ci % 6
        zslot[ci] = slot
        zrdy[ci] = wload_slot(slot, d_wz[ci])
        zq[0] += 1

    for _ in range(3):
        z_issue()
    zsig_free = [sV.n]
    for ci in range(NK):
        for g in range(2):
            off = 128 + 512 * g
            bi = (2 * ci + g) % 2
            bank = pb[bi]
            for k in range(NK):
                wts = [zrdy[ci], (sV, z_bank_free[bi])] if k == 0 else []
                T_(lambda e, k=k, ci=ci, off=off, bank=bank:
                   e.matmul(bank[:, 0:512], wbuf(zslot[ci])[:, 128 * k:128 * (k + 1)],
                            bfr(O_TT + 1152 * k + off, 512), start=(k == 0), stop=(k == NK - 1)),
                   waits=wts, inc=(sT if k == NK - 1 else None))
            S_(lambda e, bank=bank: e.activation(sigb[:, 0:512], bank[:, 0:512], AF.Sigmoid),
               waits=[(sT, sT.n), (sV, zsig_free[0])], inc=sS)
            V_(lambda e, bank=bank: e.tensor_tensor(szc[:, 0:512], bank[:, 0:512],
                                                    sigb[:, 0:512], ALU.mult),
               waits=[(sS, sS.n)], inc=sV)
            zsig_free[0] = sV.n
            z_bank_free[bi] = sV.n
            V_(lambda e, ci=ci, g=g: e.scalar_tensor_tensor(
                s1c[:, 0:512], aout[:, 1024 * ci + 512 * g:1024 * ci + 512 * g + 512],
                nwa[:, ci:ci + 1], rr[:, 512 * g:512 * g + 512], ALU.mult, ALU.mult))
            V_(lambda e, ci=ci, g=g: e.tensor_tensor(
                out2[:, 1024 * ci + 512 * g:1024 * ci + 512 * g + 512],
                s1c[:, 0:512], szc[:, 0:512], ALU.mult), inc=sV)
        wb_last_use[zslot[ci]] = sT.n
        z_issue()
    nA6_V = sV.n

    atT = bfr(O_OUT, 16384)
    wordy = {}
    woslot = {}
    woq = [0]

    def wo_issue():
        ci = woq[0]
        if ci >= NK:
            return
        slot = ci % 6
        woslot[ci] = slot
        wordy[ci] = wload_slot(slot, d_wo[ci])
        woq[0] += 1

    for _ in range(3):
        wo_issue()
    ao_bank_free = [sV.n, sV.n]
    for c in range(NK):
        for g in range(2):
            bi2 = (2 * c + g) % 2
            bank = pb[3 + bi2]
            for k in range(NK):
                wts = []
                if k == 0:
                    wts = [wordy[c], (sV, ao_bank_free[bi2])]
                    if c == 0 and g == 0:
                        wts.append((sV, nA6_V))
                T_(lambda e, k=k, g=g, bank=bank, c=c:
                   e.matmul(bank[:, 0:512], wbuf(woslot[c])[:, 128 * k:128 * (k + 1)],
                            out2[:, 1024 * k + 512 * g:1024 * k + 512 * g + 512],
                            start=(k == 0), stop=(k == NK - 1)),
                   waits=wts, inc=(sT if k == NK - 1 else None))
            V_(lambda e, c=c, g=g, bank=bank:
               e.tensor_copy(atT[:, 1024 * c + 512 * g:1024 * c + 512 * g + 512],
                             bank[:, 0:512]),
               waits=[(sT, sT.n)], inc=sV)
            ao_bank_free[bi2] = sV.n
        wb_last_use[woslot[c]] = sT.n
        wo_issue()
    nA8a_V = sV.n

    obT = bfr(O_VF, 16384)
    lnb2 = bfr(TR_LN, 2048)
    t2bf = bfr(TR_T2, 2048)
    attok = bfr(TR_AT, 2048)
    obibf = bfr(TR_OBI, 2048)
    nw2 = fr(F_NW + 16, 16)
    x1f = fr(F_X1, 2048)
    xb_users_v = [sV.n, sV.n]
    xb_users_s = [sS.n, sS.n]
    for ti in range(8):
        xslot = ti % 2
        sXp = sX0 if xslot == 0 else sX1
        xb = fr(F_XB + 2048 * xslot, 2048)
        Y_(lambda e, ti=ti, xb=xb: e.dma_start(out=xb, in_=d_x[128 + 128 * ti:256 + 128 * ti, :]),
           waits=[(sV, xb_users_v[xslot]), (sS, xb_users_s[xslot])], inc=sXp, incn=16)
        for c in range(NK):
            slot = c % 8
            b = ptr_rot[0] % 2
            bank = tbanks[b]
            wts = []
            if ti == 0 and c == 0:
                wts.append((sV, nA8a_V))
                wts.append((sS, sS.n))
            if c % 8 == 0:
                wts.append(ptr_reads[b])
            T_(lambda e, c=c, ti=ti, slot=slot, bank=bank:
               e.transpose(bank[:, 128 * slot:128 * slot + 128],
                           atT[:, 1024 * c + 128 * ti:1024 * c + 128 * ti + 128], ident),
               waits=wts, inc=(sT if c % 8 == 7 else None))
            if c % 8 == 7:
                tc = sT.n
                for q in range(8):
                    cq = c - 7 + q
                    S_(lambda e, cq=cq, q=q, bank=bank:
                       e.copy(attok[:, 128 * cq:128 * cq + 128],
                              bank[:, 128 * q:128 * q + 128]),
                       waits=[(sT, tc)] if q == 0 else [],
                       inc=(sS if q == 7 else None))
                ptr_reads[b] = (sS, sS.n)
                ptr_rot[0] += 1
        em.add("vector", lambda e, sXp=sXp, v=sXp.n: e.wait_ge(sXp.h, v))
        V_(lambda e, xb=xb: e.tensor_tensor(x1f, xb, attok, ALU.add),
           waits=[(sS, sS.n)], inc=sV)
        xb_users_v[xslot] = sV.n
        if d_yx1 is not None:
            Y_(lambda e, ti=ti: e.dma_start(out=d_yx1[128 * ti:128 * (ti + 1), :], in_=x1f),
               waits=[(sV, sV.n)], inc=sST, incn=16)
            em.add("vector", lambda e: e.nop(), waits=[(sST, sST.n)])
        ln_stats(x1f, ceps5, extra_waits=[(sV, sV.n), (sT, last_transpose[0])])
        V_(lambda e: e.tensor_scalar(lnb2, x1f, stats[:, 4:5], stats[:, 5:6],
                                     ALU.mult, ALU.add),
           waits=[(sT, last_transpose[0])], inc=sV)
        V_(lambda e: e.tensor_tensor(obibf, x1f, lnb2, ALU.add), inc=sV)
        xb_users_v[xslot] = sV.n
        S_(lambda e: e.activation(sqsc, lnb2, AF.Square, accum_out=stats[:, 1:2]),
           waits=[(sV, sV.n)], inc=sS)
        V_(lambda e: e.tensor_scalar(stats[:, 3:4], stats[:, 1:2], CINV, None, ALU.mult),
           waits=[(sS, sS.n)], inc=sV)
        S_(lambda e: e.activation(stats[:, 7:8], stats[:, 3:4], AF.Sqrt,
                                  bias=ceps5[:, 0:1], scale=1.0),
           waits=[(sV, sV.n)], inc=sS)
        V_(lambda e: e.reciprocal(stats[:, 4:5], stats[:, 7:8]), waits=[(sS, sS.n)], inc=sV)
        V_(lambda e: e.tensor_scalar(t2bf, lnb2, stats[:, 4:5], None, ALU.mult),
           waits=[(sT, last_transpose[0])], inc=sV)

        def wr_ob(kq, blk, wts, do_inc, ti=ti):
            S_(lambda e, kq=kq, blk=blk:
               e.copy(obT[:, 1024 * kq + 128 * ti:1024 * kq + 128 * ti + 128], blk),
               waits=wts, inc=(sS if do_inc else None))

        def wr_t2(kq, blk, wts, do_inc, ti=ti):
            V_(lambda e, kq=kq, blk=blk:
               e.tensor_scalar(bfr(O_TT + 1152 * kq + 128 + 128 * ti, 128), blk,
                               nw2[:, kq:kq + 1], None, ALU.mult),
               waits=wts, inc=(sV if do_inc else None))

        transposes(obibf, wr_ob, sV, sV.n, sS)
        transposes(t2bf, wr_t2, sV, sV.n, sV)

    if dbg == "x1":
        V_(lambda e: e.memset(fr(F_XB, 2048), 0.0), waits=[(sST, sST.n)], inc=sV)
        for ti in range(8):
            Y_(lambda e, ti=ti: e.dma_start(out=d_y[128 * ti:128 * (ti + 1), :],
                                            in_=fr(F_XB, 2048)),
               waits=[(sV, sV.n)] if ti == 0 else [(sST, sST.n)], inc=sST, incn=16)
        em.add("sync", lambda e: e.nop(), waits=[(sST, sST.n)])
        finalize(nc, es, em)
        return nc

    Gf = fr(F_G, 128)
    lgb = fr(F_GS, 16)
    msk = fr(F_GS + 16, 16)
    mrow = fr(F_GS + 32, 8)
    GTb = bfr(TR_GT, 1024)
    b2bf = bfr(TR_B2, 2048)
    G_(lambda e: e.dma_start(out=b2bf[0:16, :], in_=d_b2[:, :]),
       waits=[(sT, sT.n)], inc=sB2, incn=16)
    b2rdy = (sB2, sB2.n)
    pb2_free_s = [sS.n]
    for ti in range(8):
        for k in range(NK):
            T_(lambda e, k=k, ti=ti:
               e.matmul(pb[2][:, 0:16], bfr(O_TT + 1152 * k + 128 + 128 * ti, 128),
                        gwb[:, 16 * k:16 * k + 16], start=(k == 0), stop=False),
               waits=([(sV, sV.n), (sS, pb2_free_s[0])] if k == 0 else []))
        T_(lambda e: e.matmul(pb[2][:, 0:16], onesrow, gbb_sb[:, :], start=False, stop=True),
           inc=sT)
        em.add("vector", lambda e: e.nop(), waits=[(sT, sT.n)])
        V_(lambda e: e.tensor_copy(lgb[:, 0:16], pb[2][:, 0:16]))
        for it in range(4):
            V_(lambda e, it=it: e.reduce_max(mrow[:, it:it + 1], lgb[:, 0:16], axis=AX.X))
            if it < 3:
                V_(lambda e, it=it: e.tensor_single_scalar(msk[:, 0:16], lgb[:, 0:16],
                                                           mrow[:, it:it + 1], ALU.is_ge))
                V_(lambda e: e.scalar_tensor_tensor(lgb[:, 0:16], msk[:, 0:16], -1e30,
                                                    lgb[:, 0:16], ALU.mult, ALU.add))
        V_(lambda e: e.tensor_scalar(stats[:, 10:11], mrow[:, 0:1], -1.0, None, ALU.mult),
           inc=sV)
        S_(lambda e: e.activation(msk[:, 0:16], pb[2][:, 0:16], AF.Exp,
                                  bias=stats[:, 10:11], scale=1.0),
           waits=[(sV, sV.n)], inc=sS)
        pb2_free_s[0] = sS.n
        V_(lambda e: e.tensor_single_scalar(lgb[:, 0:16], pb[2][:, 0:16],
                                            mrow[:, 3:4], ALU.is_ge), waits=[(sS, sS.n)])
        V_(lambda e: e.tensor_tensor(msk[:, 0:16], msk[:, 0:16], lgb[:, 0:16], ALU.mult))
        V_(lambda e: e.reduce_sum(stats[:, 11:12], msk[:, 0:16], axis=AX.X))
        V_(lambda e: e.reciprocal(stats[:, 12:13], stats[:, 11:12]))
        V_(lambda e, ti=ti: e.tensor_single_scalar(Gf[:, 16 * ti:16 * ti + 16], msk[:, 0:16],
                                                   stats[:, 12:13], ALU.mult), inc=sV)
        V_(lambda e, ti=ti: e.tensor_copy(bfr(TR_GB, 128)[:, 0:16],
                                          Gf[:, 16 * ti:16 * ti + 16]), inc=sV)
        T_(lambda e: e.transpose(ptr[0:16, 0:128], bfr(TR_GB, 128)[:, 0:16], ident),
           waits=[(sV, sV.n), ptr_reads[0]], inc=sT)
        V_(lambda e, ti=ti: e.tensor_copy(GTb[0:16, 128 * ti:128 * (ti + 1)], ptr[0:16, 0:128]),
           waits=[(sT, sT.n)], inc=sV)
        ptr_reads[0] = (sV, sV.n)
    nB2_V = sV.n

    nB1_obS = sS.n
    b3_free = [sV.n, sV.n]
    for c in range(NK):
        for g in range(2):
            bi2 = (2 * c + g) % 2
            T_(lambda e, c=c, g=g, bi2=bi2:
               e.matmul(pb[3 + bi2][:, 0:512], b2bf[0:16, 128 * c:128 * (c + 1)],
                        GTb[0:16, 512 * g:512 * g + 512], start=True, stop=True),
               waits=[b2rdy, (sV, nB2_V), (sV, b3_free[bi2])], inc=sT)
            V_(lambda e, c=c, g=g, bi2=bi2: e.tensor_tensor(
                obT[:, 1024 * c + 512 * g:1024 * c + 512 * g + 512],
                obT[:, 1024 * c + 512 * g:1024 * c + 512 * g + 512],
                pb[3 + bi2][:, 0:512], ALU.add),
               waits=[(sT, sT.n)] + ([(sS, nB1_obS)] if c == 0 and g == 0 else []),
               inc=sV)
            b3_free[bi2] = sV.n

    act = bfr(O_OUT, 16384)
    y1b = bfr(O_TR + 2048, 512)
    sab = bfr(TR_SA, 512)
    bbb = bfr(TR_BB, 512)
    q2g = bfr(TR_Q2G, 512)
    Gbb = bfr(TR_GBB, 1024)
    h_free_v = [sV.n] * 4
    h_free_s = [sS.n] * 4
    y_free_v = [sV.n] * 3
    wslot_i = [0]

    def next_slot():
        s = wslot_i[0] % 6
        wslot_i[0] += 1
        return s

    act_read_t = [sT.n]
    gbb_free_v = [sV.n]
    for e_ in range(E):
        T_(lambda e, e_=e_: e.matmul(pb[2][:, 0:512], er_sb[:, 128 * e_:128 * (e_ + 1)],
                                     GTb[0:16, 0:512], start=True, stop=True),
           waits=[(sV, sV.n), (sS, pb2_free_s[0])])
        T_(lambda e, e_=e_: e.matmul(pb[6][:, 0:512], er_sb[:, 128 * e_:128 * (e_ + 1)],
                                     GTb[0:16, 512:1024], start=True, stop=True), inc=sT)
        V_(lambda e: e.tensor_copy(Gbb[:, 0:512], pb[2][:, 0:512]), waits=[(sT, sT.n)])
        V_(lambda e: e.tensor_copy(Gbb[:, 512:1024], pb[6][:, 0:512]), inc=sV)
        gbb_free_v[0] = sV.n
        h_free_v[2] = max(h_free_v[2], sV.n)
        y_free_v[2] = max(y_free_v[2], sV.n)

        for m in range(NK):
            sa_slot = next_slot()
            rdy_a = wload_slot(sa_slot, d_w1a[e_, m])
            sb_slot = next_slot()
            rdy_b = wload_slot(sb_slot, d_w1b[e_, m])
            for g in range(2):
                off = 128 + 512 * g
                iA = (2 * (2 * m + g)) % 4
                iB = iA + 1
                bA, bB = pb[iA], pb[iB]
                for k in range(NK):
                    wts = [rdy_a, (sV, h_free_v[iA]), (sS, h_free_s[iA])] if k == 0 else []
                    T_(lambda e, k=k, off=off, bA=bA, sa_slot=sa_slot:
                       e.matmul(bA[:, 0:512], wbuf(sa_slot)[:, 128 * k:128 * (k + 1)],
                                bfr(O_TT + 1152 * k + off, 512),
                                start=(k == 0), stop=(k == NK - 1)),
                       waits=wts, inc=(sT if k == NK - 1 else None))
                tA = sT.n
                for k in range(NK):
                    wts = [rdy_b, (sV, h_free_v[iB])] if k == 0 else []
                    T_(lambda e, k=k, off=off, bB=bB, sb_slot=sb_slot:
                       e.matmul(bB[:, 0:512], wbuf(sb_slot)[:, 128 * k:128 * (k + 1)],
                                bfr(O_TT + 1152 * k + off, 512),
                                start=(k == 0), stop=(k == NK - 1)),
                       waits=wts, inc=(sT if k == NK - 1 else None))
                tB = sT.n
                wb_last_use[sa_slot] = tA
                wb_last_use[sb_slot] = tB
                col = 16 * e_ + m
                S_(lambda e, bA=bA, col=col: e.activation(sab[:, 0:512], bA[:, 0:512], AF.Relu,
                                                          bias=c7ma[:, col:col + 1], scale=-1.0),
                   waits=[(sT, tA), (sV, sV.n)], inc=sS)
                S_(lambda e: e.activation(sab[:, 0:512], sab[:, 0:512], AF.Silu,
                                          bias=csilu[:, 0:1], scale=-ALPHA), inc=sS)
                h_free_s[iA] = sS.n
                V_(lambda e, bB=bB, col=col: e.tensor_scalar(bbb[:, 0:512], bB[:, 0:512],
                                                             b1b[:, col:col + 1], -LIMIT,
                                                             ALU.add, ALU.max),
                   waits=[(sT, tB), (sS, sS.n)], inc=sV)
                h_free_v[iB] = sV.n
                V_(lambda e: e.tensor_scalar(q2g[:, 0:512], bbb[:, 0:512], LIMIT, KA,
                                             ALU.min, ALU.mult))
                act_w = [(sS, sS.n)]
                if m == 0 and g == 0:
                    act_w.append((sT, act_read_t[0]))
                V_(lambda e, m=m, g=g: e.scalar_tensor_tensor(
                    act[:, 1024 * m + 512 * g:1024 * m + 512 * g + 512],
                    q2g[:, 0:512], KA, sab[:, 0:512], ALU.add, ALU.mult),
                   waits=act_w, inc=sV)
                h_free_v[iA] = sV.n
        act_done = sV.n
        yq = [0]
        w2rdy = {}
        w2slot = {}

        def w2_issue(e_=e_, yq=yq, w2rdy=w2rdy, w2slot=w2slot):
            ci = yq[0]
            if ci >= NK:
                return
            slot = next_slot()
            w2slot[ci] = slot
            w2rdy[ci] = wload_slot(slot, d_w2[e_, ci])
            yq[0] += 1

        for _ in range(3):
            w2_issue()
        for c in range(NK):
            for g in range(2):
                yi = (2 * c + g) % 3
                bank = pb[4 + yi]
                for k in range(NK):
                    wts = []
                    if k == 0:
                        wts = [w2rdy[c], (sV, y_free_v[yi])]
                        if c == 0 and g == 0:
                            wts.append((sV, act_done))
                    T_(lambda e, k=k, g=g, bank=bank, c=c, w2slot=w2slot:
                       e.matmul(bank[:, 0:512], wbuf(w2slot[c])[:, 128 * k:128 * (k + 1)],
                                act[:, 1024 * k + 512 * g:1024 * k + 512 * g + 512],
                                start=(k == 0), stop=(k == NK - 1)),
                       waits=wts, inc=(sT if k == NK - 1 else None))
                V_(lambda e, g=g, bank=bank:
                   e.tensor_tensor(y1b[:, 0:512], bank[:, 0:512],
                                   Gbb[:, 512 * g:512 * g + 512], ALU.mult),
                   waits=[(sT, sT.n)])
                V_(lambda e, c=c, g=g: e.tensor_tensor(
                    obT[:, 1024 * c + 512 * g:1024 * c + 512 * g + 512],
                    obT[:, 1024 * c + 512 * g:1024 * c + 512 * g + 512],
                    y1b[:, 0:512], ALU.add), inc=sV)
                y_free_v[yi] = sV.n
            wb_last_use[w2slot[c]] = sT.n
            w2_issue()
        act_read_t[0] = sT.n

    ys = fr(F_YS, 2048)
    ys_dma = [0]
    for ti in range(8):
        for c in range(NK):
            slot = c % 8
            b = ptr_rot[0] % 2
            bank = tbanks[b]
            wts = []
            if ti == 0 and c == 0:
                wts.append((sV, sV.n))
                wts.append((sS, sS.n))
            if c % 8 == 0:
                wts = wts + [ptr_reads[b]]
            T_(lambda e, c=c, ti=ti, slot=slot, bank=bank:
               e.transpose(bank[:, 128 * slot:128 * slot + 128],
                           obT[:, 1024 * c + 128 * ti:1024 * c + 128 * ti + 128], ident),
               waits=wts, inc=(sT if c % 8 == 7 else None))
            if c % 8 == 7:
                tc = sT.n
                for q in range(8):
                    cq = c - 7 + q
                    wts2 = [(sT, tc)] if q == 0 else []
                    if c == 7 and q == 0:
                        wts2.append((sST, ys_dma[0]))
                    S_(lambda e, cq=cq, q=q, bank=bank:
                       e.copy(ys[:, 128 * cq:128 * cq + 128],
                              bank[:, 128 * q:128 * q + 128]),
                       waits=wts2, inc=(sS if q == 7 else None))
                ptr_reads[b] = (sS, sS.n)
                ptr_rot[0] += 1
        Y_(lambda e, ti=ti: e.dma_start(out=d_y[128 * ti:128 * (ti + 1), :], in_=ys),
           waits=[(sS, sS.n)], inc=sST, incn=16)
        ys_dma[0] = sST.n
    em.add("sync", lambda e: e.nop(), waits=[(sST, sST.n)])

    finalize(nc, es, em)
    return nc


def finalize(nc, es, em):
    with es:
        block = es.enter_context(nc.Block())
        block.tensor(lambda e: [f(e) for f in em.q["tensor"]])
        block.vector(lambda e: [f(e) for f in em.q["vector"]])
        block.scalar(lambda e: [f(e) for f in em.q["scalar"]])
        block.gpsimd(lambda e: [f(e) for f in em.q["gpsimd"]])
        block.sync(lambda e: [f(e) for f in em.q["sync"]])



def prep_inputs(d):
    xf = np.ascontiguousarray(np.asarray(d["L0_attn_x"], dtype=np.float32).reshape(B * S, HID))

    def tile_w(W):
        return np.ascontiguousarray(
            W.reshape(NK, 128, NK, 128).transpose(2, 1, 0, 3).reshape(NK, 128, 2048))

    Wq = tile_w(np.asarray(d["L0_attn_Wq"], dtype=np.float32))
    Wk = tile_w(np.asarray(d["L0_attn_Wk"], dtype=np.float32))
    Wv = tile_w(np.asarray(d["L0_attn_Wv"], dtype=np.float32))
    Wz = tile_w(np.asarray(d["L0_attn_Wz"], dtype=np.float32))
    Wo = tile_w(np.asarray(d["L0_attn_Wout"], dtype=np.float32))
    Wa = np.asarray(d["L0_attn_Wa"], dtype=np.float32).reshape(NK, 128, 16).transpose(1, 0, 2).reshape(128, 256)
    Wb = np.asarray(d["L0_attn_Wb"], dtype=np.float32).reshape(NK, 128, 16).transpose(1, 0, 2).reshape(128, 256)
    wab = np.ascontiguousarray(np.concatenate([Wa, Wb], axis=1))
    conv = np.ascontiguousarray(np.concatenate([
        np.asarray(d[f"L0_attn_conv_{w}_w"], dtype=np.float32)
        .reshape(NK, 128, 4).transpose(1, 0, 2).reshape(128, 64)
        for w in ("q", "k", "v")], axis=1))
    nw = np.ascontiguousarray(np.concatenate([
        np.asarray(d["L0_attn_norm_w"], dtype=np.float32).reshape(NK, 128).T,
        np.asarray(d["L0_ffn_norm_w"], dtype=np.float32).reshape(NK, 128).T], axis=1))
    E16c = np.zeros((128, 256), dtype=BFNP)
    for h in range(16):
        E16c[:, 16 * h + h] = 1
    eone = np.zeros((128, 224), dtype=BFNP)
    eone[:, 128] = 1
    cb = np.concatenate([E16c, np.eye(128, dtype=BFNP), np.ones((128, 128), dtype=BFNP),
                         eone], axis=1)
    er = np.zeros((16, 2048), dtype=BFNP)
    for h in range(16):
        er[h, 128 * h:128 * (h + 1)] = 1
    gw = np.ascontiguousarray(
        np.asarray(d["L0_ffn_gate_w"], dtype=np.float32).T
        .reshape(NK, 128, 16).transpose(1, 0, 2).reshape(128, 256))
    gb = np.ascontiguousarray(np.asarray(d["L0_ffn_gate_b"], dtype=np.float32).reshape(1, 16))
    b1 = np.asarray(d["L0_ffn_b1"], dtype=np.float32)
    b1a = b1[:, 0::2].reshape(E, NK, 128).transpose(2, 0, 1).reshape(128, 256)
    b1bv = b1[:, 1::2].reshape(E, NK, 128).transpose(2, 0, 1).reshape(128, 256)
    b1t = np.ascontiguousarray(np.concatenate([b1a, b1bv], axis=1))
    b2 = np.ascontiguousarray(np.asarray(d["L0_ffn_b2"], dtype=np.float32))
    w1 = np.asarray(d["L0_ffn_w1"], dtype=np.float32)

    def tile_w1(w1h):
        return np.ascontiguousarray(
            w1h.reshape(E, NK, 128, NK, 128).transpose(0, 1, 4, 3, 2).reshape(E, NK, 128, 2048))

    w1a = tile_w1(w1[:, 0::2, :])
    w1bt = tile_w1(w1[:, 1::2, :])
    w2 = np.asarray(d["L0_ffn_w2"], dtype=np.float32)
    w2t = np.ascontiguousarray(
        w2.reshape(E, NK, 128, NK, 128).transpose(0, 1, 4, 3, 2).reshape(E, NK, 128, 2048))

    inv = (1.0 / (1e6 ** (np.arange(0, D, 2, dtype=np.float32) / np.float32(D)))).astype(np.float32)
    in_maps = []
    for core in range(NCORES):
        g0 = T * core
        if core % 2 == 1:
            halo = xf[g0 - HALO:g0]
        else:
            halo = np.zeros((HALO, HID), dtype=np.float32)
        xsh = np.ascontiguousarray(np.concatenate([halo, xf[g0:g0 + T]], axis=0))
        pos0 = (core % 2) * T
        pos = np.arange(pos0 - HALO, pos0 + T, dtype=np.float32)
        frm = (pos[None, :] * inv[:, None]).astype(np.float32)
        cos64 = np.cos(frm).astype(np.float32)
        sin64 = np.sin(frm).astype(np.float32)
        cos_t = np.ascontiguousarray(np.concatenate([cos64, cos64], axis=0))
        s2_t = np.ascontiguousarray(np.concatenate([-sin64, sin64], axis=0))
        in_maps.append({
            "xsh": xsh, "wq_t": Wq, "wk_t": Wk, "wv_t": Wv, "wz_t": Wz,
            "wo_t": Wo, "wab_t": wab, "conv_t": conv, "nw_t": nw,
            "cos_t": cos_t, "s2_t": s2_t, "cb_t": cb, "er_t": er,
            "gw_t": gw, "gb_t": gb, "b1_t": b1t, "b2_t": b2,
            "w1a_t": w1a, "w1b_t": w1bt, "w2_t": w2t,
        })
    return in_maps


def kernel(**inputs):
    nc = build(selfchain=("vector",))
    in_maps = prep_inputs(inputs)
    res = run_bass_kernel_spmd(nc, in_maps, core_ids=list(range(NCORES)))
    out = np.concatenate([np.asarray(r["y"], dtype=np.float32) for r in res.results], axis=0)
    return out.reshape(B, S, HID)


# version 08
# speedup vs baseline: 1.1119x; 1.1119x over previous
import numpy as np
import ml_dtypes
import concourse.bass as bass
import concourse.mybir as mybir
from concourse.bass_utils import run_bass_kernel_spmd
from contextlib import ExitStack

BFNP = ml_dtypes.bfloat16
F32 = mybir.dt.float32
BF16 = mybir.dt.bfloat16
AF = mybir.ActivationFunctionType
ALU = mybir.AluOpType
AX = mybir.AxisListType

NCORES = 8
B, S, HID = 4, 2048, 2048
H, D = 16, 128
E = 16
T = 1024
HALO = 128
TT_ = HALO + T
NK = HID // 128
SCALE = 1.0 / float(np.sqrt(D))
ALPHA = 1.702
KA = float(1.0 / ALPHA)
LIMIT = 7.0
G3 = [(0, 512), (512, 512), (1024, 128)]
CINV = 1.0 / HID

O_TT = 0
O_VF = 18432
O_OUT = 36864
O_TR = 53248
O_WB = 62976
O_SM = 75264
TR_Q2, TR_Q, TR_K2, TR_K, TR_VP, TR_SH, TR_AC = (O_TR + i * 1152 for i in range(7))
TR_SIG = O_TR + 7 * 1152
TR_T1 = TR_SIG + 512
TR_WRA = O_TR
TR_WRB = O_TR + 1024
TR_SZ = O_TR + 2048
TR_S1 = TR_SZ + 512
TR_SQ2 = TR_S1 + 512
TR_LN = O_TR
TR_T2 = O_TR + 2048
TR_AT = O_TR + 4096
TR_OBI = O_TR + 6144
TR_GB = O_TR
TR_SA = O_TR + 4096
TR_BB = TR_SA + 512
TR_Q2G = TR_BB + 512
TR_GBB = TR_Q2G + 512
TR_B2 = TR_GBB + 1024
TR_GT = O_TR + 8704
SM_E16 = O_SM
SM_ID = O_SM + 256
SM_ONESM = O_SM + 384
SM_GW = O_SM + 512
SM_ONESR = O_SM + 768
SM_WAB = O_SM + 896
SM_EONE = O_SM + 1408
SH2 = O_SM + 1632
F2G = O_SM + 2784
BF_COLS = O_SM + 3296

F_XB = 0
F_ROWS = 4096
F_RR = 6400
F_CONV = 7424
F_NW = 7616
F_STAT = 7648
F_MISC = 7680
F_F1 = 7744
F_F2 = F_RR
F_B1 = 8256
F_D = 9024
F_X1 = F_D
F32_COLS = F_D + 2560
F_G = F_RR
F_GS = F_RR + 128
F_YS = F_XB


class Sem:
    def __init__(self, es, nc, name):
        self.h = es.enter_context(nc.semaphore(name))
        self.n = 0


class Emit:
    KEYS = ("tensor", "vector", "scalar", "gpsimd", "sync")

    def __init__(self):
        self.q = {k: [] for k in self.KEYS}
        self.selfs = None
        self.marker = {}

    def add(self, key, fn, waits=(), inc=None, incn=1, dma=False):
        waits = [(s, v) for (s, v) in waits if v > 0]
        chain = self.selfs is not None and key in self.selfs and not dma
        selfw = self.marker.get(key) if chain else None
        use_self = chain and inc is None
        if chain:
            if inc is None:
                ss = self.selfs[key]
                ss.n += 1
                self.marker[key] = (ss, ss.n)
            else:
                self.marker[key] = (inc, inc.n + incn)

        def f(eng, waits=waits, fn=fn, inc=inc, incn=incn, selfw=selfw,
              use_self=use_self, key=key):
            for (s, v) in waits:
                eng.wait_ge(s.h, v)
            if selfw is not None:
                eng.wait_ge(selfw[0].h, selfw[1])
            ins = fn(eng)
            if ins is None:
                ins = eng.nop()
            if inc is not None:
                ins.then_inc(inc.h, incn)
            if use_self:
                ins.then_inc(self.selfs[key].h, 1)
        self.q[key].append(f)
        if inc is not None:
            inc.n += incn


def build(dbg=None, selfchain=False):
    if selfchain is True:
        selfchain = ("tensor", "vector", "scalar")
    elif not selfchain:
        selfchain = ()
    nc = bass.Bass("TRN2", target_bir_lowering=False,
                   detect_race_conditions="tensor" in selfchain)

    d_x = nc.dram_tensor("xsh", [TT_, HID], F32, kind="ExternalInput").ap()
    d_wq = nc.dram_tensor("wq_t", [NK, 128, 2048], F32, kind="ExternalInput").ap()
    d_wk = nc.dram_tensor("wk_t", [NK, 128, 2048], F32, kind="ExternalInput").ap()
    d_wv = nc.dram_tensor("wv_t", [NK, 128, 2048], F32, kind="ExternalInput").ap()
    d_wz = nc.dram_tensor("wz_t", [NK, 128, 2048], F32, kind="ExternalInput").ap()
    d_wo = nc.dram_tensor("wo_t", [NK, 128, 2048], F32, kind="ExternalInput").ap()
    d_wab = nc.dram_tensor("wab_t", [128, 512], F32, kind="ExternalInput").ap()
    d_conv = nc.dram_tensor("conv_t", [128, 192], F32, kind="ExternalInput").ap()
    d_nw = nc.dram_tensor("nw_t", [128, 32], F32, kind="ExternalInput").ap()
    d_cos = nc.dram_tensor("cos_t", [128, TT_], F32, kind="ExternalInput").ap()
    d_s2 = nc.dram_tensor("s2_t", [128, TT_], F32, kind="ExternalInput").ap()
    d_cb = nc.dram_tensor("cb_t", [128, 736], BF16, kind="ExternalInput").ap()
    d_er = nc.dram_tensor("er_t", [16, 2048], BF16, kind="ExternalInput").ap()
    d_gw = nc.dram_tensor("gw_t", [128, 256], F32, kind="ExternalInput").ap()
    d_gb = nc.dram_tensor("gb_t", [1, 16], F32, kind="ExternalInput").ap()
    d_b1 = nc.dram_tensor("b1_t", [128, 512], F32, kind="ExternalInput").ap()
    d_b2 = nc.dram_tensor("b2_t", [16, 2048], F32, kind="ExternalInput").ap()
    d_w1a = nc.dram_tensor("w1a_t", [E, NK, 128, 2048], F32, kind="ExternalInput").ap()
    d_w1b = nc.dram_tensor("w1b_t", [E, NK, 128, 2048], F32, kind="ExternalInput").ap()
    d_w2 = nc.dram_tensor("w2_t", [E, NK, 128, 2048], F32, kind="ExternalInput").ap()
    d_y = nc.dram_tensor("y", [T, HID], F32, kind="ExternalOutput").ap()
    d_yx1 = None
    if dbg == "x1":
        d_yx1 = nc.dram_tensor("yx1", [T, HID], F32, kind="ExternalOutput").ap()

    es = ExitStack()
    bf = es.enter_context(nc.sbuf_tensor("bfA", [128, BF_COLS], BF16))
    fA = es.enter_context(nc.sbuf_tensor("f32A", [128, F32_COLS], F32))
    er_sb = es.enter_context(nc.sbuf_tensor("er_sb", [16, 2048], BF16))
    gb_sb = es.enter_context(nc.sbuf_tensor("gb_sb", [1, 16], F32))
    gbb_sb = es.enter_context(nc.sbuf_tensor("gbb_sb", [1, 16], BF16))
    pb = [es.enter_context(nc.psum_tensor(f"pb{i}", [128, 512], F32)) for i in range(7)]
    ptr = es.enter_context(nc.psum_tensor("ptr", [128, 1024], BF16))
    vf32 = bf.bitcast(F32)

    def bfr(off, n):
        return bf[:, off:off + n]

    def fr(off, n):
        return fA[:, off:off + n]

    def wbuf(i):
        return bf[:, O_WB + 2048 * i:O_WB + 2048 * (i + 1)]

    sWs = [Sem(es, nc, f"sW{i}") for i in range(6)]
    sX0 = Sem(es, nc, "sX0")
    sX1 = Sem(es, nc, "sX1")
    sXR = Sem(es, nc, "sXR")
    sB2 = Sem(es, nc, "sB2")
    sT = Sem(es, nc, "sT")
    sV = Sem(es, nc, "sV")
    sS = Sem(es, nc, "sS")
    sST = Sem(es, nc, "sST")
    sGC = Sem(es, nc, "sGC")

    em = Emit()
    if selfchain:
        selfs = {k: Sem(es, nc, f"self_{k}") for k in selfchain}
        em.selfs = selfs
    T_ = lambda fn, waits=(), inc=None, incn=1: em.add("tensor", fn, waits, inc, incn)
    V_ = lambda fn, waits=(), inc=None, incn=1: em.add("vector", fn, waits, inc, incn)
    S_ = lambda fn, waits=(), inc=None, incn=1: em.add("scalar", fn, waits, inc, incn)
    G_ = lambda fn, waits=(), inc=None, incn=1: em.add("gpsimd", fn, waits, inc, incn,
                                                      dma=True)
    GC_ = lambda fn, waits=(), inc=None, incn=1: em.add("gpsimd", fn, waits, inc, incn)
    Y_ = lambda fn, waits=(), inc=None, incn=1: em.add("sync", fn, waits, inc, incn,
                                                      dma=True)

    ident = bfr(SM_ID, 128)
    E16 = bfr(SM_E16, 256)
    onesmat = bfr(SM_ONESM, 128)
    onesrow = bfr(SM_ONESR, 128)[0:1, :]
    wab_bf = bfr(SM_WAB, 512)
    gwb = bfr(SM_GW, 256)
    stats = fr(F_STAT, 32)
    convw = fr(F_CONV, 192)
    C_A = F_ROWS
    C_B = F_ROWS + 1152
    cosbf = bfr(O_OUT, TT_)
    s2bf = bfr(O_OUT + TT_, TT_)
    sqsc = bfr(TR_T2, 2048)

    Y_(lambda e: e.dma_start(out=bfr(SM_E16, 512), in_=d_cb[:, 0:512]), inc=sX0, incn=16)
    Y_(lambda e: e.dma_start(out=bfr(SM_EONE, 224), in_=d_cb[:, 512:736]), inc=sX0, incn=16)
    Y_(lambda e: e.dma_start(out=er_sb[:, :], in_=d_er[:, :]), inc=sX0, incn=16)
    VFF = O_VF // 2
    Y_(lambda e: e.dma_start(out=vf32[:, VFF:VFF + TT_], in_=d_cos[:, :]), inc=sX0, incn=16)
    Y_(lambda e: e.dma_start(out=vf32[:, VFF + TT_:VFF + 2 * TT_], in_=d_s2[:, :]), inc=sX0, incn=16)
    Y_(lambda e: e.dma_start(out=convw, in_=d_conv[:, :]), inc=sX0, incn=16)
    Y_(lambda e: e.dma_start(out=fr(F_NW, 32), in_=d_nw[:, :]), inc=sX0, incn=16)
    Y_(lambda e: e.dma_start(out=fr(F_XB, 512), in_=d_wab[:, :]), inc=sX0, incn=16)
    Y_(lambda e: e.dma_start(out=fr(F_XB + 512, 256), in_=d_gw[:, :]), inc=sX0, incn=16)
    Y_(lambda e: e.dma_start(out=gb_sb[:, :], in_=d_gb[:, :]), inc=sX0, incn=16)
    Y_(lambda e: e.dma_start(out=fr(F_B1, 512), in_=d_b1[:, :]), inc=sX0, incn=16)
    nLoads = sX0.n

    V_(lambda e: e.tensor_copy(cosbf, vf32[:, VFF:VFF + TT_]), waits=[(sX0, nLoads)])
    V_(lambda e: e.tensor_copy(s2bf, vf32[:, VFF + TT_:VFF + 2 * TT_]))
    V_(lambda e: e.tensor_copy(wab_bf, fr(F_XB, 512)))
    V_(lambda e: e.tensor_copy(gwb, fr(F_XB + 512, 256)))
    V_(lambda e: e.tensor_copy(gbb_sb[:, :], gb_sb[:, :]))
    V_(lambda e: e.memset(bfr(SM_ONESR, 128)[0:1, :], 1.0))
    V_(lambda e: e.memset(fr(F_MISC, 1), 1e-5))
    V_(lambda e: e.memset(fr(F_MISC + 1, 1), 1e-6))
    V_(lambda e: e.memset(fr(F_MISC + 2, 1), ALPHA * LIMIT))
    V_(lambda e: e.tensor_scalar(fr(F_B1 + 512, 256), fr(F_B1, 256), -1.0, LIMIT,
                                 ALU.mult, ALU.add))
    V_(lambda e: e.memset(fr(F_D, 2560), 0.0), inc=sV)
    nConstV = sV.n
    ceps5 = fr(F_MISC, 1)
    ceps6 = fr(F_MISC + 1, 1)
    csilu = fr(F_MISC + 2, 1)
    c7ma = fr(F_B1 + 512, 256)
    b1b = fr(F_B1 + 256, 256)

    ptrB = pb[0].bitcast(BF16)
    tbanks = [ptr, ptrB]
    ptr_reads = [(sV, 0), (sV, 0)]
    ptr_rot = [0]
    last_transpose = [0]

    def ln_stats(src_ap, eps_ap, extra_waits=(), so=0, sq=None):
        if sq is None:
            sq = sqsc
        S_(lambda e: e.activation(src_ap, src_ap, AF.Copy, accum_out=stats[:, so:so + 1]),
           waits=list(extra_waits), inc=sS)
        S_(lambda e, sq=sq: e.activation(sq, src_ap, AF.Square,
                                         accum_out=stats[:, so + 1:so + 2]),
           inc=sS)
        V_(lambda e: e.tensor_scalar(stats[:, so + 2:so + 3], stats[:, so:so + 1],
                                     CINV, None, ALU.mult),
           waits=[(sS, sS.n)])
        V_(lambda e: e.tensor_scalar(stats[:, so + 3:so + 4], stats[:, so + 1:so + 2],
                                     CINV, None, ALU.mult))
        V_(lambda e: e.tensor_tensor(stats[:, so + 6:so + 7], stats[:, so + 2:so + 3],
                                     stats[:, so + 2:so + 3], ALU.mult))
        V_(lambda e: e.tensor_tensor(stats[:, so + 3:so + 4], stats[:, so + 3:so + 4],
                                     stats[:, so + 6:so + 7], ALU.subtract),
           inc=sV)
        S_(lambda e: e.activation(stats[:, so + 7:so + 8], stats[:, so + 3:so + 4], AF.Sqrt,
                                  bias=eps_ap[:, 0:1], scale=1.0),
           waits=[(sV, sV.n)], inc=sS)
        V_(lambda e: e.reciprocal(stats[:, so + 4:so + 5], stats[:, so + 7:so + 8]),
           waits=[(sS, sS.n)])
        V_(lambda e: e.scalar_tensor_tensor(stats[:, so + 5:so + 6], stats[:, so + 2:so + 3],
                                            -1.0, stats[:, so + 4:so + 5],
                                            ALU.mult, ALU.mult), inc=sV)

    def transposes(src_bf, dst_writer, src_sem, src_val, wsem, extra0=()):
        for kk in range(NK):
            slot = kk % 8
            b = ptr_rot[0] % 2
            bank = tbanks[b]
            wts = [(src_sem, src_val)] + list(extra0) if kk == 0 else []
            if kk % 8 == 0:
                wts = wts + [ptr_reads[b]]
            T_(lambda e, kk=kk, slot=slot, src_bf=src_bf, bank=bank:
               e.transpose(bank[:, 128 * slot:128 * slot + 128],
                           src_bf[:, 128 * kk:128 * (kk + 1)], ident),
               waits=wts, inc=(sT if kk % 8 == 7 else None))
            if kk % 8 == 7:
                tcount = sT.n
                guard = None
                for q in range(8):
                    g = dst_writer(kk - 7 + q, bank[:, 128 * q:128 * q + 128],
                                   [(sT, tcount)] if q == 0 else [], q == 7)
                    if g is not None:
                        guard = g
                ptr_reads[b] = guard if guard is not None else (wsem, wsem.n)
                ptr_rot[0] += 1
        last_transpose[0] = sT.n

    xb_users_v = [nConstV, nConstV]
    xb_users_s = [0, 0]
    T_(lambda e: e.nop(), waits=[(sX0, nLoads)])
    for i in range(9):
        xslot = i % 2
        sXp = sX0 if xslot == 0 else sX1
        xb = fr(F_XB + 2048 * xslot, 2048)
        Y_(lambda e, i=i, xb=xb: e.dma_start(out=xb, in_=d_x[128 * i:128 * (i + 1), :]),
           waits=[(sV, xb_users_v[xslot]), (sS, xb_users_s[xslot])], inc=sXp, incn=16)
        so = 16 * xslot
        sq = sqsc if xslot == 0 else bfr(O_VF, 2048)
        ln_stats(xb, ceps5, extra_waits=[(sXp, sXp.n)], so=so, sq=sq)
        xb_users_s[xslot] = sS.n
        lnbf = bfr(TR_LN if xslot == 0 else TR_OBI, 2048)
        V_(lambda e, xb=xb, so=so, lnbf=lnbf: e.tensor_scalar(
            lnbf, xb, stats[:, so + 4:so + 5], stats[:, so + 5:so + 6],
            ALU.mult, ALU.add),
           waits=[(sT, last_transpose[0])], inc=sV)
        xb_users_v[xslot] = sV.n

        def wr_tt(kq, blk, wts, do_inc, i=i):
            if (kq // 8) == 0:
                S_(lambda e, kq=kq, blk=blk:
                   e.copy(bfr(O_TT + 1152 * kq + 128 * i, 128), blk),
                   waits=wts, inc=(sS if do_inc else None))
                return (sS, sS.n) if do_inc else None
            V_(lambda e, kq=kq, blk=blk:
               e.tensor_copy(bfr(O_TT + 1152 * kq + 128 * i, 128), blk),
               waits=wts, inc=(sV if do_inc else None))
            return (sV, sV.n) if do_inc else None
        transposes(lnbf, wr_tt, sV, sV.n, sS)

    ab_evac = [sV.n]
    for gi, (off, n) in enumerate(G3):
        for k in range(NK):
            T_(lambda e, k=k, off=off, n=n:
               e.matmul(pb[5][0:16, 0:n], wab_bf[:, 16 * k:16 * k + 16],
                        bfr(O_TT + 1152 * k + off, n), start=(k == 0), stop=(k == NK - 1)),
               waits=([(sV, sV.n), (sS, sS.n)] if (gi == 0 and k == 0) else
                      [(sV, ab_evac[0])] if k == 0 else []))
            T_(lambda e, k=k, off=off, n=n:
               e.matmul(pb[6][0:16, 0:n], wab_bf[:, 256 + 16 * k:256 + 16 * k + 16],
                        bfr(O_TT + 1152 * k + off, n), start=(k == 0), stop=(k == NK - 1)),
               inc=(sT if k == NK - 1 else None))
        V_(lambda e, off=off, n=n: e.tensor_copy(fA[0:16, C_A + off:C_A + off + n],
                                                 pb[5][0:16, 0:n]),
           waits=[(sT, sT.n)])
        V_(lambda e, off=off, n=n: e.tensor_copy(fA[0:16, C_B + off:C_B + off + n],
                                                 pb[6][0:16, 0:n]),
           inc=sV)
        ab_evac[0] = sV.n
    ab_free = sV.n

    q2b, qb, k2b, kb, vpre = (bfr(o, 1152) for o in (TR_Q2, TR_Q, TR_K2, TR_K, TR_VP))
    shb = bfr(TR_SH, 1152)
    acc1 = bfr(TR_AC, 1152)
    sigb = bfr(TR_SIG, 512)
    t1b = bfr(TR_T1, 512)
    D1, D2, D3, D4, D5 = (fr(F_D + 512 * i, 512) for i in range(5))

    wb_last_use = [0] * 6
    dma_ready = {}
    slot_of = {}
    order = [(h, w) for h in range(H) for w in ('v', 'q', 'k')]
    issued = [0]

    def wload_slot(slot, dram_ap):
        G_(lambda e, dram_ap=dram_ap, slot=slot: e.dma_start(out=wbuf(slot), in_=dram_ap),
           waits=[(sT, wb_last_use[slot])], inc=sWs[slot], incn=16)
        return (sWs[slot], sWs[slot].n)

    def issue_next():
        if issued[0] >= len(order):
            return
        h, which = order[issued[0]]
        slot = issued[0] % 6
        slot_of[(h, which)] = slot
        dma_ready[(h, which)] = wload_slot(slot, {'v': d_wv, 'q': d_wq, 'k': d_wk}[which][h])
        issued[0] += 1

    for _ in range(6):
        issue_next()

    proj_bank_free = [sV.n, sV.n]
    sig_free = [sV.n]
    GC_(lambda e: e.memset(shb[:, 0:128], 0.0), inc=sGC)

    pbc = [0]

    def proj(h, which, dest, war_gc=0):
        rdy = dma_ready[(h, which)]
        slot = slot_of[(h, which)]
        for gi, (off, n) in enumerate(G3):
            bi = pbc[0] % 2
            pbc[0] += 1
            bank = pb[bi]
            for k in range(NK):
                wts = [rdy, (sV, proj_bank_free[bi])] if k == 0 else []
                T_(lambda e, k=k, off=off, n=n, bank=bank, slot=slot:
                   e.matmul(bank[:, 0:n], wbuf(slot)[:, 128 * k:128 * (k + 1)],
                            bfr(O_TT + 1152 * k + off, n), start=(k == 0), stop=(k == NK - 1)),
                   waits=wts, inc=(sT if k == NK - 1 else None))
            S_(lambda e, bank=bank, n=n: e.activation(sigb[:, 0:n], bank[:, 0:n], AF.Sigmoid),
               waits=[(sT, sT.n), (sV, sig_free[0])], inc=sS)
            V_(lambda e, bank=bank, off=off, n=n, dest=dest:
               e.tensor_tensor(dest[:, off:off + n], bank[:, 0:n], sigb[:, 0:n], ALU.mult),
               waits=[(sS, sS.n)] + ([(sGC, war_gc)] if gi == 0 else []), inc=sV)
            proj_bank_free[bi] = sV.n
            sig_free[0] = sV.n
        wb_last_use[slot] = sT.n
        return sV.n

    def emit_conv(h, src, dst, wsel, rope_prev_gc=0, rope_dma_prev=0):
        base = 64 * wsel + 4 * h
        n = TT_ - 125
        w0 = ([(sGC, rope_prev_gc)] if rope_prev_gc else []) + \
             ([(sXR, rope_dma_prev)] if rope_dma_prev else [])
        V_(lambda e, src=src, dst=dst, base=base, n=n: e.tensor_scalar(
            dst[:, 125:125 + n], src[:, 122:122 + n], convw[:, base:base + 1],
            None, ALU.mult), waits=w0)
        V_(lambda e, src=src, dst=dst, base=base, n=n: e.scalar_tensor_tensor(
            dst[:, 125:125 + n], src[:, 123:123 + n], convw[:, base + 1:base + 2],
            dst[:, 125:125 + n], ALU.mult, ALU.add))
        V_(lambda e, src=src, dst=dst, base=base, n=n: e.scalar_tensor_tensor(
            dst[:, 125:125 + n], src[:, 124:124 + n], convw[:, base + 2:base + 3],
            dst[:, 125:125 + n], ALU.mult, ALU.add))
        V_(lambda e, src=src, dst=dst, base=base, n=n: e.scalar_tensor_tensor(
            dst[:, 125:125 + n], src[:, 125:125 + n], convw[:, base + 3:base + 4],
            dst[:, 125:125 + n], ALU.mult, ALU.add), inc=sV)
        return sV.n

    def emit_rope(src, dst, conv_done):
        n = TT_ - 125
        Y_(lambda e, src=src: e.dma_start(out=acc1[0:64, 125:125 + n],
                                          in_=src[64:128, 125:125 + n]),
           waits=[(sGC, sGC.n), (sV, conv_done)], inc=sXR, incn=16)
        Y_(lambda e, src=src: e.dma_start(out=acc1[64:128, 125:125 + n],
                                          in_=src[0:64, 125:125 + n]),
           inc=sXR, incn=16)
        GC_(lambda e, src=src, dst=dst: e.tensor_tensor(
            dst[:, 125:125 + n], src[:, 125:125 + n], cosbf[:, 125:125 + n], ALU.mult),
            waits=[(sV, conv_done)], inc=sGC)
        g1 = sGC.n
        GC_(lambda e: e.tensor_tensor(acc1[:, 125:125 + n], acc1[:, 125:125 + n],
                                      s2bf[:, 125:125 + n], ALU.mult),
            waits=[(sXR, sXR.n)])
        GC_(lambda e, dst=dst: e.tensor_tensor(dst[:, 125:125 + n], dst[:, 125:125 + n],
                                               acc1[:, 125:125 + n], ALU.add), inc=sGC)
        return g1, sXR.n

    dscr_free = [sV.n]

    g1q = [0]
    g1k = [0]
    ropeq_dma = [0]
    ropek_dma = [0]
    dots_done = [0]
    last_scr_dots_t = [0]
    pending = [None]
    shb2 = bfr(SH2, 1152)
    scr_total = {2: 3, 3: 3, 4: 3, 5: 3, 6: 2}
    DOTMAP2 = {(0, 0): (2, 0), (0, 1): (2, 1), (1, 0): (2, 2), (1, 1): (3, 0),
               (2, 0): (3, 1), (2, 1): (3, 2), (3, 0): (4, 0), (3, 1): (4, 1)}
    nn = TT_ - 125

    def make_scr(h):
        state = {"seen": {2: 0, 3: 0, 4: 0, 5: 0, 6: 0}, "first": True}

        def scr_mm(bank_i, slot, rhs_off, n, gc_val, src=shb):
            wts = [(sGC, gc_val)]
            if state["first"]:
                wts.append((sV, dscr_free[0]))
                if h == 0:
                    wts.append((sV, ab_free))
                state["first"] = False
            st = state["seen"][bank_i] == 0
            sp = state["seen"][bank_i] == scr_total[bank_i] - 1
            state["seen"][bank_i] += 1
            c0 = SM_EONE + 128 - (32 * slot + h)
            T_(lambda e, bank_i=bank_i, rhs_off=rhs_off, n=n, st=st, sp=sp, c0=c0,
               src=src:
               e.matmul(pb[bank_i][0:96, 0:n], bfr(c0, 96), src[:, rhs_off:rhs_off + n],
                        start=st, stop=sp, skip_group_check=True),
               waits=wts, inc=sT)
        return scr_mm

    for h in range(H):
        proj(h, 'v', vpre)
        issue_next()
        emit_conv(h, vpre, bf[:, O_VF + 1152 * h:O_VF + 1152 * (h + 1)], 2)
        if pending[0] is not None:
            pending[0]()
            pending[0] = None
        eq = proj(h, 'q', q2b, war_gc=dots_done[0])
        issue_next()
        cq = emit_conv(h, q2b, qb, 0, g1q[0], ropeq_dma[0])
        g1q[0], ropeq_dma[0] = emit_rope(qb, q2b, cq)
        GC_(lambda e: e.tensor_tensor(shb[:, 125:125 + nn], q2b[:, 125:125 + nn],
                                      q2b[:, 125:125 + nn], ALU.mult),
            waits=[(sT, sT.n)], inc=sGC)
        qq_done = sGC.n
        ek = proj(h, 'k', k2b, war_gc=dots_done[0])
        issue_next()
        ck = emit_conv(h, k2b, kb, 1, g1k[0], ropek_dma[0])
        g1k[0], ropek_dma[0] = emit_rope(kb, k2b, ck)
        scr = make_scr(h)
        scr(4, 2, 0, 512, qq_done)
        scr(5, 0, 512, 512, qq_done)
        scr(5, 1, 1024, 128, qq_done)
        scrqq_t = sT.n
        GC_(lambda e: e.tensor_tensor(shb[:, 125:125 + nn], k2b[:, 125:125 + nn],
                                      k2b[:, 125:125 + nn], ALU.mult),
            waits=[(sT, scrqq_t)], inc=sGC)
        kk_done = sGC.n

        def tail(h=h, scr=scr, kk_done=kk_done):
            scr(6, 0, 0, 512, kk_done)
            scr(6, 1, 512, 512, kk_done)
            scr(5, 2, 1024, 128, kk_done)
            prev_t = last_scr_dots_t[0]
            for s in range(4):
                GC_(lambda e, s=s: e.tensor_tensor(
                    shb2[:, 128:128 + T], q2b[:, 128:128 + T],
                    k2b[:, 128 - s:128 - s + T], ALU.mult),
                    waits=[(sT, prev_t)], inc=sGC)
                dval = sGC.n
                for half in range(2):
                    bank_i, slot = DOTMAP2[(s, half)]
                    scr(bank_i, slot, 128 + 512 * half, 512, dval, src=shb2)
                prev_t = sT.n
            last_scr_dots_t[0] = sT.n
            dots_done[0] = sGC.n
            em.add("vector", lambda e: e.nop(), waits=[(sT, sT.n)])
            V_(lambda e: e.tensor_tensor(D1[0:96, :], D1[0:96, :], pb[2][0:96, 0:512],
                                         ALU.add))
            V_(lambda e: e.tensor_tensor(D2[0:96, :], D2[0:96, :], pb[3][0:96, 0:512],
                                         ALU.add))
            V_(lambda e: e.tensor_tensor(D3[0:96, :], D3[0:96, :], pb[4][0:96, 0:512],
                                         ALU.add))
            V_(lambda e: e.tensor_tensor(D4[0:96, :], D4[0:96, :], pb[5][0:96, 0:512],
                                         ALU.add))
            V_(lambda e: e.tensor_tensor(D5[0:96, :], D5[0:96, :], pb[6][0:96, 0:512],
                                         ALU.add), inc=sV)
            dscr_free[0] = sV.n
        pending[0] = tail
    pending[0]()
    pending[0] = None
    nA3_V = sV.n
    nA3_GC = sGC.n

    rqsrc = [(D3, 64, 0, 512), (D4, 0, 512, 512), (D4, 32, 1024, 128)]
    rksrc = [(D5, 0, 0, 512), (D5, 32, 512, 512), (D4, 64, 1024, 128)]
    for cbase, regions in ((C_A, rqsrc), (C_B, rksrc)):
        for (dsrc, po, off, n) in regions:
            S_(lambda e, dsrc=dsrc, po=po, n=n: e.activation(
                fA[po:po + 16, F_F1:F_F1 + n], dsrc[po:po + 16, 0:n], AF.Sqrt),
               waits=[(sV, sV.n)], inc=sS)
            V_(lambda e, po=po, n=n: e.tensor_scalar(
                fA[po:po + 16, F_F1:F_F1 + n], fA[po:po + 16, F_F1:F_F1 + n],
                1e-12, None, ALU.max), waits=[(sS, sS.n)])
            V_(lambda e, cbase=cbase, po=po, off=off, n=n:
               e.reciprocal(fA[32:48, cbase + off:cbase + off + n],
                            fA[po:po + 16, F_F1:F_F1 + n]), inc=sV)

    def wrow(s):
        return bf[0:16, O_TR + 1024 * s:O_TR + 1024 * (s + 1)]

    DOTMAP = {(0, 0): (D1, 0), (0, 1): (D1, 32), (1, 0): (D1, 64),
              (1, 1): (D2, 0), (2, 0): (D2, 32), (2, 1): (D2, 64),
              (3, 0): (D3, 0), (3, 1): (D3, 32)}
    wrow_done = [None] * 4
    for s in range(4):
        for half in range(2):
            dsrc, po = DOTMAP[(s, half)]
            hb = 128 + 512 * half
            V_(lambda e, s=s, hb=hb, po=po: e.tensor_tensor(
                fA[po:po + 16, F_F1:F_F1 + 512],
                fA[0:16, C_A + hb - s:C_A + hb - s + 512],
                fA[0:16, C_B + hb:C_B + hb + 512], ALU.mult), inc=sV)
            S_(lambda e, po=po: e.activation(fA[po:po + 16, F_F1:F_F1 + 512],
                                             fA[po:po + 16, F_F1:F_F1 + 512], AF.Sigmoid),
               waits=[(sV, sV.n)], inc=sS)
            V_(lambda e, dsrc=dsrc, po=po: e.tensor_tensor(
                fA[po:po + 16, F_F2:F_F2 + 512], fA[po:po + 16, F_F1:F_F1 + 512],
                dsrc[po:po + 16, 0:512], ALU.mult), waits=[(sS, sS.n)])
            V_(lambda e, s=s, hb=hb, po=po: e.scalar_tensor_tensor(
                fA[po:po + 16, F_F1:F_F1 + 512],
                fA[32:48, C_A + hb:C_A + hb + 512], SCALE,
                fA[32:48, C_B + hb - s:C_B + hb - s + 512], ALU.mult, ALU.mult))
            V_(lambda e, s=s, half=half, po=po: e.tensor_tensor(
                wrow(s)[:, 512 * half:512 * half + 512],
                fA[po:po + 16, F_F2:F_F2 + 512], fA[po:po + 16, F_F1:F_F1 + 512],
                ALU.mult), inc=sV)
        wrow_done[s] = (sV.n, 0)
    nA4_V = sV.n

    aout = bfr(O_OUT, 16384)
    evb = bfr(TR_VP, 2048)
    t1v = bfr(TR_AC, 512)
    t1g = bfr(TR_AC + 512, 512)
    a5_bank_s = [sS.n] * 4
    a5_last_mult = [None] * 4
    first_own = {"v": True, "g": True}
    seq = [0]
    for s in range(4):
        wsrc = wrow(s)
        for h in range(H):
            for half in range(2):
                bi = seq[0] % 4
                seq[0] += 1
                bank = pb[bi]
                mm_w = [(sS, a5_bank_s[bi])]
                if h == 0 and half == 0:
                    mm_w.append((sV, wrow_done[s][0]))
                    mm_w.append((sGC, wrow_done[s][1]))
                T_(lambda e, bank=bank, h=h, wsrc=wsrc, half=half:
                   e.matmul(bank[:, 0:512], er_sb[:, 128 * h:128 * (h + 1)],
                            wsrc[:, 512 * half:512 * half + 512],
                            start=True, stop=True),
                   waits=mm_w, inc=sT)
                ev_w = [(sT, sT.n)]
                if a5_last_mult[bi] is not None:
                    ev_w.append(a5_last_mult[bi])
                S_(lambda e, bank=bank, bi=bi:
                   e.copy(evb[:, 512 * bi:512 * bi + 512], bank[:, 0:512]),
                   waits=ev_w, inc=sS)
                a5_bank_s[bi] = sS.n
                own_v = (h + half) % 2 == 0
                EN = V_ if own_v else GC_
                osem = sV if own_v else sGC
                t1x = t1v if own_v else t1g
                vcol = O_VF + 1152 * h + 128 + 512 * half - s
                ocol = O_OUT + 1024 * h + 512 * half
                mw = [(sS, sS.n)]
                if own_v and first_own["v"]:
                    mw.append((sGC, nA3_GC))
                    first_own["v"] = False
                if not own_v and first_own["g"]:
                    first_own["g"] = False
                if s == 0:
                    EN(lambda e, bi=bi, vcol=vcol, ocol=ocol:
                       e.tensor_tensor(bf[:, ocol:ocol + 512],
                                       evb[:, 512 * bi:512 * bi + 512],
                                       bf[:, vcol:vcol + 512], ALU.mult),
                       waits=mw, inc=osem)
                else:
                    EN(lambda e, bi=bi, vcol=vcol, t1x=t1x:
                       e.tensor_tensor(t1x[:, 0:512],
                                       evb[:, 512 * bi:512 * bi + 512],
                                       bf[:, vcol:vcol + 512], ALU.mult),
                       waits=mw, inc=osem)
                    EN(lambda e, ocol=ocol, t1x=t1x: e.tensor_tensor(
                        bf[:, ocol:ocol + 512], bf[:, ocol:ocol + 512],
                        t1x[:, 0:512], ALU.add), inc=osem)
                a5_last_mult[bi] = (osem, osem.n - (0 if s == 0 else 1))
    nA5_V = sV.n
    nA5_GC = sGC.n
    nA5_S = sS.n

    rr = fr(F_RR, 1024)
    sqb2 = bfr(TR_SQ2, 512)
    rr_mm_t = [0]
    for g in range(2):
        for ch in range(NK):
            S_(lambda e, ch=ch, g=g: e.activation(
                sqb2[:, 0:512], aout[:, 1024 * ch + 512 * g:1024 * ch + 512 * g + 512],
                AF.Square),
               waits=[(sV, nA5_V), (sGC, nA5_GC), (sT, rr_mm_t[0])], inc=sS)
            T_(lambda e, ch=ch: e.matmul(pb[2][:, 0:512], onesmat, sqb2[:, 0:512],
                                         start=(ch == 0), stop=(ch == NK - 1)),
               waits=[(sS, sS.n)] + ([(sV, sV.n)] if (g == 1 and ch == 0) else []),
               inc=sT)
            rr_mm_t[0] = sT.n
        S_(lambda e: e.activation(fr(F_F1, 512)[:, 0:512], pb[2][:, 0:512], AF.Sqrt,
                                  bias=ceps6[:, 0:1], scale=CINV),
           waits=[(sT, sT.n)], inc=sS)
        V_(lambda e, g=g: e.reciprocal(rr[:, 512 * g:512 * g + 512], fr(F_F1, 512)[:, 0:512]),
           waits=[(sS, sS.n)], inc=sV)

    out2 = bfr(O_VF, 16384)
    nwa = fr(F_NW, 32)
    szc = bfr(TR_SZ, 512)
    s1c = bfr(TR_S1, 512)
    z_bank_free = [sV.n, sV.n]
    zrdy = {}
    zslot = {}
    zq = [0]

    def z_issue():
        ci = zq[0]
        if ci >= NK:
            return
        slot = ci % 6
        zslot[ci] = slot
        zrdy[ci] = wload_slot(slot, d_wz[ci])
        zq[0] += 1

    for _ in range(3):
        z_issue()
    zsig_free = [sV.n]
    for ci in range(NK):
        for g in range(2):
            off = 128 + 512 * g
            bi = (2 * ci + g) % 2
            bank = pb[bi]
            for k in range(NK):
                wts = [zrdy[ci], (sV, z_bank_free[bi])] if k == 0 else []
                if ci == 0 and g == 0 and k == 0:
                    wts.append((sS, nA5_S))
                T_(lambda e, k=k, ci=ci, off=off, bank=bank:
                   e.matmul(bank[:, 0:512], wbuf(zslot[ci])[:, 128 * k:128 * (k + 1)],
                            bfr(O_TT + 1152 * k + off, 512), start=(k == 0), stop=(k == NK - 1)),
                   waits=wts, inc=(sT if k == NK - 1 else None))
            S_(lambda e, bank=bank: e.activation(sigb[:, 0:512], bank[:, 0:512], AF.Sigmoid),
               waits=[(sT, sT.n), (sV, zsig_free[0])], inc=sS)
            V_(lambda e, bank=bank: e.tensor_tensor(szc[:, 0:512], bank[:, 0:512],
                                                    sigb[:, 0:512], ALU.mult),
               waits=[(sS, sS.n)] + ([(sGC, nA5_GC)] if ci == 0 and g == 0 else []),
               inc=sV)
            zsig_free[0] = sV.n
            z_bank_free[bi] = sV.n
            V_(lambda e, ci=ci, g=g: e.scalar_tensor_tensor(
                s1c[:, 0:512], aout[:, 1024 * ci + 512 * g:1024 * ci + 512 * g + 512],
                nwa[:, ci:ci + 1], rr[:, 512 * g:512 * g + 512], ALU.mult, ALU.mult))
            V_(lambda e, ci=ci, g=g: e.tensor_tensor(
                out2[:, 1024 * ci + 512 * g:1024 * ci + 512 * g + 512],
                s1c[:, 0:512], szc[:, 0:512], ALU.mult), inc=sV)
        wb_last_use[zslot[ci]] = sT.n
        z_issue()
    nA6_V = sV.n

    atT = bfr(O_OUT, 16384)
    wordy = {}
    woslot = {}
    woq = [0]

    def wo_issue():
        ci = woq[0]
        if ci >= NK:
            return
        slot = ci % 6
        woslot[ci] = slot
        wordy[ci] = wload_slot(slot, d_wo[ci])
        woq[0] += 1

    for _ in range(3):
        wo_issue()
    ao_bank_free = [sV.n, sV.n]
    for c in range(NK):
        for g in range(2):
            bi2 = (2 * c + g) % 2
            bank = pb[3 + bi2]
            for k in range(NK):
                wts = []
                if k == 0:
                    wts = [wordy[c], (sV, ao_bank_free[bi2])]
                    if c == 0 and g == 0:
                        wts.append((sV, nA6_V))
                T_(lambda e, k=k, g=g, bank=bank, c=c:
                   e.matmul(bank[:, 0:512], wbuf(woslot[c])[:, 128 * k:128 * (k + 1)],
                            out2[:, 1024 * k + 512 * g:1024 * k + 512 * g + 512],
                            start=(k == 0), stop=(k == NK - 1)),
                   waits=wts, inc=(sT if k == NK - 1 else None))
            V_(lambda e, c=c, g=g, bank=bank:
               e.tensor_copy(atT[:, 1024 * c + 512 * g:1024 * c + 512 * g + 512],
                             bank[:, 0:512]),
               waits=[(sT, sT.n)], inc=sV)
            ao_bank_free[bi2] = sV.n
        wb_last_use[woslot[c]] = sT.n
        wo_issue()
    nA8a_V = sV.n

    obT = bfr(O_VF, 16384)
    lnb2 = bfr(TR_LN, 2048)
    t2bf = bfr(TR_T2, 2048)
    attok = bfr(TR_AT, 2048)
    obibf = bfr(TR_OBI, 2048)
    nw2 = fr(F_NW + 16, 16)
    x1f = fr(F_X1, 2048)
    xb_users_v = [sV.n, sV.n]
    xb_users_s = [sS.n, sS.n]
    for ti in range(8):
        xslot = ti % 2
        sXp = sX0 if xslot == 0 else sX1
        xb = fr(F_XB + 2048 * xslot, 2048)
        Y_(lambda e, ti=ti, xb=xb: e.dma_start(out=xb, in_=d_x[128 + 128 * ti:256 + 128 * ti, :]),
           waits=[(sV, xb_users_v[xslot]), (sS, xb_users_s[xslot])], inc=sXp, incn=16)
        for c in range(NK):
            slot = c % 8
            b = ptr_rot[0] % 2
            bank = tbanks[b]
            wts = []
            if ti == 0 and c == 0:
                wts.append((sV, nA8a_V))
                wts.append((sS, sS.n))
            if c % 8 == 0:
                wts.append(ptr_reads[b])
            T_(lambda e, c=c, ti=ti, slot=slot, bank=bank:
               e.transpose(bank[:, 128 * slot:128 * slot + 128],
                           atT[:, 1024 * c + 128 * ti:1024 * c + 128 * ti + 128], ident),
               waits=wts, inc=(sT if c % 8 == 7 else None))
            if c % 8 == 7:
                tc = sT.n
                for q in range(8):
                    cq = c - 7 + q
                    S_(lambda e, cq=cq, q=q, bank=bank:
                       e.copy(attok[:, 128 * cq:128 * cq + 128],
                              bank[:, 128 * q:128 * q + 128]),
                       waits=[(sT, tc)] if q == 0 else [],
                       inc=(sS if q == 7 else None))
                ptr_reads[b] = (sS, sS.n)
                ptr_rot[0] += 1
        em.add("vector", lambda e, sXp=sXp, v=sXp.n: e.wait_ge(sXp.h, v))
        V_(lambda e, xb=xb: e.tensor_tensor(x1f, xb, attok, ALU.add),
           waits=[(sS, sS.n)], inc=sV)
        xb_users_v[xslot] = sV.n
        if d_yx1 is not None:
            Y_(lambda e, ti=ti: e.dma_start(out=d_yx1[128 * ti:128 * (ti + 1), :], in_=x1f),
               waits=[(sV, sV.n)], inc=sST, incn=16)
            em.add("vector", lambda e: e.nop(), waits=[(sST, sST.n)])
        ln_stats(x1f, ceps5, extra_waits=[(sV, sV.n), (sT, last_transpose[0])])
        V_(lambda e: e.tensor_scalar(lnb2, x1f, stats[:, 4:5], stats[:, 5:6],
                                     ALU.mult, ALU.add),
           waits=[(sT, last_transpose[0])], inc=sV)
        V_(lambda e: e.tensor_tensor(obibf, x1f, lnb2, ALU.add), inc=sV)
        xb_users_v[xslot] = sV.n
        S_(lambda e: e.activation(sqsc, lnb2, AF.Square, accum_out=stats[:, 1:2]),
           waits=[(sV, sV.n)], inc=sS)
        V_(lambda e: e.tensor_scalar(stats[:, 3:4], stats[:, 1:2], CINV, None, ALU.mult),
           waits=[(sS, sS.n)], inc=sV)
        S_(lambda e: e.activation(stats[:, 7:8], stats[:, 3:4], AF.Sqrt,
                                  bias=ceps5[:, 0:1], scale=1.0),
           waits=[(sV, sV.n)], inc=sS)
        V_(lambda e: e.reciprocal(stats[:, 4:5], stats[:, 7:8]), waits=[(sS, sS.n)], inc=sV)
        V_(lambda e: e.tensor_scalar(t2bf, lnb2, stats[:, 4:5], None, ALU.mult),
           waits=[(sT, last_transpose[0])], inc=sV)

        def wr_ob(kq, blk, wts, do_inc, ti=ti):
            S_(lambda e, kq=kq, blk=blk:
               e.copy(obT[:, 1024 * kq + 128 * ti:1024 * kq + 128 * ti + 128], blk),
               waits=wts, inc=(sS if do_inc else None))

        def wr_t2(kq, blk, wts, do_inc, ti=ti):
            V_(lambda e, kq=kq, blk=blk:
               e.tensor_scalar(bfr(O_TT + 1152 * kq + 128 + 128 * ti, 128), blk,
                               nw2[:, kq:kq + 1], None, ALU.mult),
               waits=wts, inc=(sV if do_inc else None))

        transposes(obibf, wr_ob, sV, sV.n, sS)
        transposes(t2bf, wr_t2, sV, sV.n, sV)

    if dbg == "x1":
        V_(lambda e: e.memset(fr(F_XB, 2048), 0.0), waits=[(sST, sST.n)], inc=sV)
        for ti in range(8):
            Y_(lambda e, ti=ti: e.dma_start(out=d_y[128 * ti:128 * (ti + 1), :],
                                            in_=fr(F_XB, 2048)),
               waits=[(sV, sV.n)] if ti == 0 else [(sST, sST.n)], inc=sST, incn=16)
        em.add("sync", lambda e: e.nop(), waits=[(sST, sST.n)])
        finalize(nc, es, em)
        return nc

    Gf = fr(F_G, 128)
    lgb = fr(F_GS, 16)
    msk = fr(F_GS + 16, 16)
    mrow = fr(F_GS + 32, 8)
    GTb = bfr(TR_GT, 1024)
    b2bf = bfr(TR_B2, 2048)
    G_(lambda e: e.dma_start(out=b2bf[0:16, :], in_=d_b2[:, :]),
       waits=[(sT, sT.n)], inc=sB2, incn=16)
    b2rdy = (sB2, sB2.n)
    nB1_V = sV.n
    gate_bank = [pb[2], pb[3]]
    gate_s = [sS.n, sS.n]
    gate_v = [nB1_V, nB1_V]
    for ti in range(8):
        p = ti % 2
        gb_ = gate_bank[p]
        so = 16 * p
        lg = fr(F_GS + 64 * p, 16)
        mk = fr(F_GS + 64 * p + 16, 16)
        mr = fr(F_GS + 64 * p + 32, 8)
        gbt = bfr(TR_GB + 128 * p, 128)
        for k in range(NK):
            T_(lambda e, k=k, ti=ti, gb_=gb_:
               e.matmul(gb_[:, 0:16], bfr(O_TT + 1152 * k + 128 + 128 * ti, 128),
                        gwb[:, 16 * k:16 * k + 16], start=(k == 0), stop=False),
               waits=([(sV, gate_v[p]), (sS, gate_s[p])] if k == 0 else []))
        T_(lambda e, gb_=gb_: e.matmul(gb_[:, 0:16], onesrow, gbb_sb[:, :],
                                       start=False, stop=True),
           inc=sT)
        em.add("vector", lambda e: e.nop(), waits=[(sT, sT.n)])
        V_(lambda e, gb_=gb_, lg=lg: e.tensor_copy(lg[:, 0:16], gb_[:, 0:16]))
        for it in range(4):
            V_(lambda e, it=it, lg=lg, mr=mr:
               e.reduce_max(mr[:, it:it + 1], lg[:, 0:16], axis=AX.X))
            if it < 3:
                V_(lambda e, it=it, lg=lg, mk=mk, mr=mr:
                   e.tensor_single_scalar(mk[:, 0:16], lg[:, 0:16],
                                          mr[:, it:it + 1], ALU.is_ge))
                V_(lambda e, lg=lg, mk=mk: e.scalar_tensor_tensor(
                    lg[:, 0:16], mk[:, 0:16], -1e30, lg[:, 0:16], ALU.mult, ALU.add))
        V_(lambda e, mr=mr, so=so: e.tensor_scalar(stats[:, so + 10:so + 11], mr[:, 0:1],
                                                   -1.0, None, ALU.mult),
           inc=sV)
        S_(lambda e, gb_=gb_, mk=mk, so=so: e.activation(
            mk[:, 0:16], gb_[:, 0:16], AF.Exp,
            bias=stats[:, so + 10:so + 11], scale=1.0),
           waits=[(sV, sV.n)], inc=sS)
        gate_s[p] = sS.n
        V_(lambda e, gb_=gb_, lg=lg, mr=mr: e.tensor_single_scalar(
            lg[:, 0:16], gb_[:, 0:16], mr[:, 3:4], ALU.is_ge), waits=[(sS, sS.n)])
        V_(lambda e, lg=lg, mk=mk: e.tensor_tensor(mk[:, 0:16], mk[:, 0:16],
                                                   lg[:, 0:16], ALU.mult))
        V_(lambda e, mk=mk, so=so: e.reduce_sum(stats[:, so + 11:so + 12], mk[:, 0:16],
                                                axis=AX.X))
        V_(lambda e, so=so: e.reciprocal(stats[:, so + 12:so + 13],
                                         stats[:, so + 11:so + 12]))
        V_(lambda e, ti=ti, mk=mk, so=so: e.tensor_single_scalar(
            Gf[:, 16 * ti:16 * ti + 16], mk[:, 0:16],
            stats[:, so + 12:so + 13], ALU.mult), inc=sV)
        gate_v[p] = sV.n
        V_(lambda e, ti=ti, gbt=gbt: e.tensor_copy(gbt[:, 0:16],
                                                   Gf[:, 16 * ti:16 * ti + 16]), inc=sV)
        T_(lambda e, gbt=gbt: e.transpose(ptr[0:16, 0:128], gbt[:, 0:16], ident),
           waits=[(sV, sV.n), ptr_reads[0]], inc=sT)
        V_(lambda e, ti=ti: e.tensor_copy(GTb[0:16, 128 * ti:128 * (ti + 1)], ptr[0:16, 0:128]),
           waits=[(sT, sT.n)], inc=sV)
        ptr_reads[0] = (sV, sV.n)
    nB2_V = sV.n

    nB1_obS = sS.n
    b3_free = [sV.n, sV.n]
    for c in range(NK):
        for g in range(2):
            bi2 = (2 * c + g) % 2
            T_(lambda e, c=c, g=g, bi2=bi2:
               e.matmul(pb[3 + bi2][:, 0:512], b2bf[0:16, 128 * c:128 * (c + 1)],
                        GTb[0:16, 512 * g:512 * g + 512], start=True, stop=True),
               waits=[b2rdy, (sV, nB2_V), (sV, b3_free[bi2])]
                     + ([(sS, sS.n)] if c == 0 and g == 0 else []), inc=sT)
            V_(lambda e, c=c, g=g, bi2=bi2: e.tensor_tensor(
                obT[:, 1024 * c + 512 * g:1024 * c + 512 * g + 512],
                obT[:, 1024 * c + 512 * g:1024 * c + 512 * g + 512],
                pb[3 + bi2][:, 0:512], ALU.add),
               waits=[(sT, sT.n)] + ([(sS, nB1_obS)] if c == 0 and g == 0 else []),
               inc=sV)
            b3_free[bi2] = sV.n

    act = bfr(O_OUT, 16384)
    y1b = bfr(O_TR + 2048, 512)
    sab = bfr(TR_SA, 512)
    bbb = bfr(TR_BB, 512)
    q2g = bfr(TR_Q2G, 512)
    Gbb = bfr(TR_GBB, 1024)
    h_free_v = [sV.n] * 4
    h_free_s = [sS.n] * 4
    y_free_v = [sV.n] * 3
    wslot_i = [0]

    def next_slot():
        s = wslot_i[0] % 6
        wslot_i[0] += 1
        return s

    act_read_t = [sT.n]
    gbb_free_v = [sV.n]
    for e_ in range(E):
        T_(lambda e, e_=e_: e.matmul(pb[2][:, 0:512], er_sb[:, 128 * e_:128 * (e_ + 1)],
                                     GTb[0:16, 0:512], start=True, stop=True),
           waits=[(sV, sV.n), (sS, gate_s[0])])
        T_(lambda e, e_=e_: e.matmul(pb[6][:, 0:512], er_sb[:, 128 * e_:128 * (e_ + 1)],
                                     GTb[0:16, 512:1024], start=True, stop=True), inc=sT)
        V_(lambda e: e.tensor_copy(Gbb[:, 0:512], pb[2][:, 0:512]), waits=[(sT, sT.n)])
        V_(lambda e: e.tensor_copy(Gbb[:, 512:1024], pb[6][:, 0:512]), inc=sV)
        gbb_free_v[0] = sV.n
        h_free_v[2] = max(h_free_v[2], sV.n)
        y_free_v[2] = max(y_free_v[2], sV.n)

        for m in range(NK):
            sa_slot = next_slot()
            rdy_a = wload_slot(sa_slot, d_w1a[e_, m])
            sb_slot = next_slot()
            rdy_b = wload_slot(sb_slot, d_w1b[e_, m])
            for g in range(2):
                off = 128 + 512 * g
                iA = (2 * (2 * m + g)) % 4
                iB = iA + 1
                bA, bB = pb[iA], pb[iB]
                for k in range(NK):
                    wts = [rdy_a, (sV, h_free_v[iA]), (sS, h_free_s[iA])] if k == 0 else []
                    T_(lambda e, k=k, off=off, bA=bA, sa_slot=sa_slot:
                       e.matmul(bA[:, 0:512], wbuf(sa_slot)[:, 128 * k:128 * (k + 1)],
                                bfr(O_TT + 1152 * k + off, 512),
                                start=(k == 0), stop=(k == NK - 1)),
                       waits=wts, inc=(sT if k == NK - 1 else None))
                tA = sT.n
                for k in range(NK):
                    wts = [rdy_b, (sV, h_free_v[iB])] if k == 0 else []
                    T_(lambda e, k=k, off=off, bB=bB, sb_slot=sb_slot:
                       e.matmul(bB[:, 0:512], wbuf(sb_slot)[:, 128 * k:128 * (k + 1)],
                                bfr(O_TT + 1152 * k + off, 512),
                                start=(k == 0), stop=(k == NK - 1)),
                       waits=wts, inc=(sT if k == NK - 1 else None))
                tB = sT.n
                wb_last_use[sa_slot] = tA
                wb_last_use[sb_slot] = tB
                col = 16 * e_ + m
                S_(lambda e, bA=bA, col=col: e.activation(sab[:, 0:512], bA[:, 0:512], AF.Relu,
                                                          bias=c7ma[:, col:col + 1], scale=-1.0),
                   waits=[(sT, tA), (sV, sV.n)], inc=sS)
                S_(lambda e: e.activation(sab[:, 0:512], sab[:, 0:512], AF.Silu,
                                          bias=csilu[:, 0:1], scale=-ALPHA), inc=sS)
                h_free_s[iA] = sS.n
                V_(lambda e, bB=bB, col=col: e.tensor_scalar(bbb[:, 0:512], bB[:, 0:512],
                                                             b1b[:, col:col + 1], -LIMIT,
                                                             ALU.add, ALU.max),
                   waits=[(sT, tB), (sS, sS.n)], inc=sV)
                h_free_v[iB] = sV.n
                V_(lambda e: e.tensor_scalar(q2g[:, 0:512], bbb[:, 0:512], LIMIT, KA,
                                             ALU.min, ALU.mult))
                act_w = [(sS, sS.n)]
                if m == 0 and g == 0:
                    act_w.append((sT, act_read_t[0]))
                V_(lambda e, m=m, g=g: e.scalar_tensor_tensor(
                    act[:, 1024 * m + 512 * g:1024 * m + 512 * g + 512],
                    q2g[:, 0:512], KA, sab[:, 0:512], ALU.add, ALU.mult),
                   waits=act_w, inc=sV)
                h_free_v[iA] = sV.n
        act_done = sV.n
        yq = [0]
        w2rdy = {}
        w2slot = {}

        def w2_issue(e_=e_, yq=yq, w2rdy=w2rdy, w2slot=w2slot):
            ci = yq[0]
            if ci >= NK:
                return
            slot = next_slot()
            w2slot[ci] = slot
            w2rdy[ci] = wload_slot(slot, d_w2[e_, ci])
            yq[0] += 1

        for _ in range(3):
            w2_issue()
        for c in range(NK):
            for g in range(2):
                yi = (2 * c + g) % 3
                bank = pb[4 + yi]
                for k in range(NK):
                    wts = []
                    if k == 0:
                        wts = [w2rdy[c], (sV, y_free_v[yi])]
                        if c == 0 and g == 0:
                            wts.append((sV, act_done))
                    T_(lambda e, k=k, g=g, bank=bank, c=c, w2slot=w2slot:
                       e.matmul(bank[:, 0:512], wbuf(w2slot[c])[:, 128 * k:128 * (k + 1)],
                                act[:, 1024 * k + 512 * g:1024 * k + 512 * g + 512],
                                start=(k == 0), stop=(k == NK - 1)),
                       waits=wts, inc=(sT if k == NK - 1 else None))
                V_(lambda e, g=g, bank=bank:
                   e.tensor_tensor(y1b[:, 0:512], bank[:, 0:512],
                                   Gbb[:, 512 * g:512 * g + 512], ALU.mult),
                   waits=[(sT, sT.n)])
                V_(lambda e, c=c, g=g: e.tensor_tensor(
                    obT[:, 1024 * c + 512 * g:1024 * c + 512 * g + 512],
                    obT[:, 1024 * c + 512 * g:1024 * c + 512 * g + 512],
                    y1b[:, 0:512], ALU.add), inc=sV)
                y_free_v[yi] = sV.n
            wb_last_use[w2slot[c]] = sT.n
            w2_issue()
        act_read_t[0] = sT.n

    ys = fr(F_YS, 2048)
    ys_dma = [0]
    for ti in range(8):
        for c in range(NK):
            slot = c % 8
            b = ptr_rot[0] % 2
            bank = tbanks[b]
            wts = []
            if ti == 0 and c == 0:
                wts.append((sV, sV.n))
                wts.append((sS, sS.n))
            if c % 8 == 0:
                wts = wts + [ptr_reads[b]]
            T_(lambda e, c=c, ti=ti, slot=slot, bank=bank:
               e.transpose(bank[:, 128 * slot:128 * slot + 128],
                           obT[:, 1024 * c + 128 * ti:1024 * c + 128 * ti + 128], ident),
               waits=wts, inc=(sT if c % 8 == 7 else None))
            if c % 8 == 7:
                tc = sT.n
                use_v = (c // 8) % 2 == 1
                for q in range(8):
                    cq = c - 7 + q
                    wts2 = [(sT, tc)] if q == 0 else []
                    if q == 0:
                        wts2.append((sST, ys_dma[0]))
                    if use_v:
                        V_(lambda e, cq=cq, q=q, bank=bank:
                           e.tensor_copy(ys[:, 128 * cq:128 * cq + 128],
                                         bank[:, 128 * q:128 * q + 128]),
                           waits=wts2, inc=(sV if q == 7 else None))
                    else:
                        S_(lambda e, cq=cq, q=q, bank=bank:
                           e.copy(ys[:, 128 * cq:128 * cq + 128],
                                  bank[:, 128 * q:128 * q + 128]),
                           waits=wts2, inc=(sS if q == 7 else None))
                ptr_reads[b] = (sV, sV.n) if use_v else (sS, sS.n)
                ptr_rot[0] += 1
        Y_(lambda e, ti=ti: e.dma_start(out=d_y[128 * ti:128 * (ti + 1), :], in_=ys),
           waits=[(sS, sS.n), (sV, sV.n)], inc=sST, incn=16)
        ys_dma[0] = sST.n
    em.add("sync", lambda e: e.nop(), waits=[(sST, sST.n)])

    finalize(nc, es, em)
    return nc


def finalize(nc, es, em):
    with es:
        block = es.enter_context(nc.Block())
        block.tensor(lambda e: [f(e) for f in em.q["tensor"]])
        block.vector(lambda e: [f(e) for f in em.q["vector"]])
        block.scalar(lambda e: [f(e) for f in em.q["scalar"]])
        block.gpsimd(lambda e: [f(e) for f in em.q["gpsimd"]])
        block.sync(lambda e: [f(e) for f in em.q["sync"]])



def prep_inputs(d):
    xf = np.ascontiguousarray(np.asarray(d["L0_attn_x"], dtype=np.float32).reshape(B * S, HID))

    def tile_w(W):
        return np.ascontiguousarray(
            W.reshape(NK, 128, NK, 128).transpose(2, 1, 0, 3).reshape(NK, 128, 2048))

    Wq = tile_w(np.asarray(d["L0_attn_Wq"], dtype=np.float32))
    Wk = tile_w(np.asarray(d["L0_attn_Wk"], dtype=np.float32))
    Wv = tile_w(np.asarray(d["L0_attn_Wv"], dtype=np.float32))
    Wz = tile_w(np.asarray(d["L0_attn_Wz"], dtype=np.float32))
    Wo = tile_w(np.asarray(d["L0_attn_Wout"], dtype=np.float32))
    Wa = np.asarray(d["L0_attn_Wa"], dtype=np.float32).reshape(NK, 128, 16).transpose(1, 0, 2).reshape(128, 256)
    Wb = np.asarray(d["L0_attn_Wb"], dtype=np.float32).reshape(NK, 128, 16).transpose(1, 0, 2).reshape(128, 256)
    wab = np.ascontiguousarray(np.concatenate([Wa, Wb], axis=1))
    conv = np.ascontiguousarray(np.concatenate([
        np.asarray(d[f"L0_attn_conv_{w}_w"], dtype=np.float32)
        .reshape(NK, 128, 4).transpose(1, 0, 2).reshape(128, 64)
        for w in ("q", "k", "v")], axis=1))
    nw = np.ascontiguousarray(np.concatenate([
        np.asarray(d["L0_attn_norm_w"], dtype=np.float32).reshape(NK, 128).T,
        np.asarray(d["L0_ffn_norm_w"], dtype=np.float32).reshape(NK, 128).T], axis=1))
    E16c = np.zeros((128, 256), dtype=BFNP)
    for h in range(16):
        E16c[:, 16 * h + h] = 1
    eone = np.zeros((128, 224), dtype=BFNP)
    eone[:, 128] = 1
    cb = np.concatenate([E16c, np.eye(128, dtype=BFNP), np.ones((128, 128), dtype=BFNP),
                         eone], axis=1)
    er = np.zeros((16, 2048), dtype=BFNP)
    for h in range(16):
        er[h, 128 * h:128 * (h + 1)] = 1
    gw = np.ascontiguousarray(
        np.asarray(d["L0_ffn_gate_w"], dtype=np.float32).T
        .reshape(NK, 128, 16).transpose(1, 0, 2).reshape(128, 256))
    gb = np.ascontiguousarray(np.asarray(d["L0_ffn_gate_b"], dtype=np.float32).reshape(1, 16))
    b1 = np.asarray(d["L0_ffn_b1"], dtype=np.float32)
    b1a = b1[:, 0::2].reshape(E, NK, 128).transpose(2, 0, 1).reshape(128, 256)
    b1bv = b1[:, 1::2].reshape(E, NK, 128).transpose(2, 0, 1).reshape(128, 256)
    b1t = np.ascontiguousarray(np.concatenate([b1a, b1bv], axis=1))
    b2 = np.ascontiguousarray(np.asarray(d["L0_ffn_b2"], dtype=np.float32))
    w1 = np.asarray(d["L0_ffn_w1"], dtype=np.float32)

    def tile_w1(w1h):
        return np.ascontiguousarray(
            w1h.reshape(E, NK, 128, NK, 128).transpose(0, 1, 4, 3, 2).reshape(E, NK, 128, 2048))

    w1a = tile_w1(w1[:, 0::2, :])
    w1bt = tile_w1(w1[:, 1::2, :])
    w2 = np.asarray(d["L0_ffn_w2"], dtype=np.float32)
    w2t = np.ascontiguousarray(
        w2.reshape(E, NK, 128, NK, 128).transpose(0, 1, 4, 3, 2).reshape(E, NK, 128, 2048))

    inv = (1.0 / (1e6 ** (np.arange(0, D, 2, dtype=np.float32) / np.float32(D)))).astype(np.float32)
    in_maps = []
    for core in range(NCORES):
        g0 = T * core
        if core % 2 == 1:
            halo = xf[g0 - HALO:g0]
        else:
            halo = np.zeros((HALO, HID), dtype=np.float32)
        xsh = np.ascontiguousarray(np.concatenate([halo, xf[g0:g0 + T]], axis=0))
        pos0 = (core % 2) * T
        pos = np.arange(pos0 - HALO, pos0 + T, dtype=np.float32)
        frm = (pos[None, :] * inv[:, None]).astype(np.float32)
        cos64 = np.cos(frm).astype(np.float32)
        sin64 = np.sin(frm).astype(np.float32)
        cos_t = np.ascontiguousarray(np.concatenate([cos64, cos64], axis=0))
        s2_t = np.ascontiguousarray(np.concatenate([-sin64, sin64], axis=0))
        in_maps.append({
            "xsh": xsh, "wq_t": Wq, "wk_t": Wk, "wv_t": Wv, "wz_t": Wz,
            "wo_t": Wo, "wab_t": wab, "conv_t": conv, "nw_t": nw,
            "cos_t": cos_t, "s2_t": s2_t, "cb_t": cb, "er_t": er,
            "gw_t": gw, "gb_t": gb, "b1_t": b1t, "b2_t": b2,
            "w1a_t": w1a, "w1b_t": w1bt, "w2_t": w2t,
        })
    return in_maps


def kernel(**inputs):
    nc = build(selfchain=("vector",))
    in_maps = prep_inputs(inputs)
    res = run_bass_kernel_spmd(nc, in_maps, core_ids=list(range(NCORES)))
    out = np.concatenate([np.asarray(r["y"], dtype=np.float32) for r in res.results], axis=0)
    return out.reshape(B, S, HID)
